# version 37
# baseline (speedup 1.0000x reference)
import numpy as np
import concourse.bass as bass
import concourse.mybir as mybir
from concourse.bass_utils import run_bass_kernel_spmd

F32 = mybir.dt.float32
BF16 = mybir.dt.bfloat16
AF = mybir.ActivationFunctionType
OP = mybir.AluOpType
AX = mybir.AxisListType.X

NCORES = 8
BPC = 4
SEQ = 2048
D = 1024
INNER = 2048
HID = 4096
TC = 512
NSUB = TC // 128
KCH = SEQ // TC
NCHUNK = BPC * KCH
EPS = 1e-5
NRING = 7

ENGS = ("sync", "tensor", "vector", "scalar", "gpsimd")


class Sem:
    def __init__(self, handle):
        self.h = handle
        self.n = 0


class Sched:
    def __init__(self):
        self.ops = {e: [] for e in ENGS}

    def add(self, eng, fn, waits=(), inc=None, dma=False, drain=False):
        amt = 16 if dma else 1
        incs = []
        if inc is not None:
            for s in (inc if isinstance(inc, (list, tuple)) else [inc]):
                s.n += amt
                incs.append((s, amt))
        self.ops[eng].append((fn, tuple(waits), tuple(incs), drain))
        return incs[0][0].n if incs else None

    def replay(self, eng_name, eng):
        floor = {}
        for fn, waits, incs, drain in self.ops[eng_name]:
            for s, v in waits:
                if s == "CELL":
                    assert v[0] is not None, "unresolved ring free cell"
                    pairs = v[0]
                else:
                    pairs = [(s, v)]
                for s2, v2 in pairs:
                    if floor.get(id(s2), 0) < v2:
                        eng.wait_ge(s2.h, v2)
                        floor[id(s2)] = v2
            inst = fn(eng)
            for s, amt in incs:
                inst.then_inc(s.h, amt)
            if drain:
                eng.drain()


def build_program():
    nc = bass.Bass("TRN2", target_bir_lowering=False, debug=False)

    x_in = nc.dram_tensor("L0_attn_x", [BPC, SEQ, D], F32, kind="ExternalInput").ap()
    w_norm = nc.dram_tensor("L0_attn_w_norm", [D], F32, kind="ExternalInput").ap()
    W_in = nc.dram_tensor("L0_attn_W_in", [2 * INNER, D], F32, kind="ExternalInput").ap()
    W_dt = nc.dram_tensor("L0_attn_W_dt", [INNER, INNER], F32, kind="ExternalInput").ap()
    b_dt = nc.dram_tensor("L0_attn_b_dt", [INNER], F32, kind="ExternalInput").ap()
    W_out = nc.dram_tensor("L0_attn_W_out", [D, INNER], F32, kind="ExternalInput").ap()
    gate_w = nc.dram_tensor("L0_ffn_gate_w", [2, D], F32, kind="ExternalInput").ap()
    w1 = nc.dram_tensor("L0_ffn_w1", [2, D, HID], F32, kind="ExternalInput").ap()
    b1 = nc.dram_tensor("L0_ffn_b1", [2, HID], F32, kind="ExternalInput").ap()
    w2 = nc.dram_tensor("L0_ffn_w2", [2, HID, D], F32, kind="ExternalInput").ap()
    b2 = nc.dram_tensor("L0_ffn_b2", [2, D], F32, kind="ExternalInput").ap()
    sw1 = nc.dram_tensor("L0_ffn_sw1", [D, HID], F32, kind="ExternalInput").ap()
    sb1 = nc.dram_tensor("L0_ffn_sb1", [HID], F32, kind="ExternalInput").ap()
    sw2 = nc.dram_tensor("L0_ffn_sw2", [HID, D], F32, kind="ExternalInput").ap()
    sb2 = nc.dram_tensor("L0_ffn_sb2", [D], F32, kind="ExternalInput").ap()
    y_out = nc.dram_tensor("y", [BPC, SEQ, D], F32, kind="ExternalOutput").ap()

    t_a1 = nc.dram_tensor("t_a1", [8, 128, 4096], BF16, kind="Internal").ap()
    t_a2 = nc.dram_tensor("t_a2", [8, 128, 4096], BF16, kind="Internal").ap()
    t_c = nc.dram_tensor("t_c", [4, 128, 4096], BF16, kind="Internal").ap()
    t_f1 = nc.dram_tensor("t_f1", [3, 8, 128, 4096], BF16, kind="Internal").ap()
    t_f2 = nc.dram_tensor("t_f2", [3, 8, 128, 4096], BF16, kind="Internal").ap()

    from contextlib import ExitStack
    stack = ExitStack()
    E = stack.enter_context

    x_sb = E(nc.sbuf_tensor("x_sb", [128, NSUB, D], F32))
    x1_sb = E(nc.sbuf_tensor("x1_sb", [128, NSUB, D], F32))
    tT_sb = E(nc.sbuf_tensor("tT_sb", [128, 8, TC], BF16))
    u_sb = E(nc.sbuf_tensor("u_sb", [128, 16, TC], BF16))
    zs_sb = E(nc.sbuf_tensor("zs_sb", [128, 16, TC], BF16))
    U1 = E(nc.sbuf_tensor("U1", [128, 8192], F32))
    U2 = E(nc.sbuf_tensor("U2", [128, 8192], F32))
    ring = E(nc.sbuf_tensor("ring", [128, NRING, 4096], BF16))
    b_scr = E(nc.sbuf_tensor("b_scr", [128, 2, TC], BF16))
    b_scr2 = [b_scr[:, 0, :], b_scr[:, 1, :]]
    sq_scr = E(nc.sbuf_tensor("sq_scr", [128, D], BF16))
    idb = E(nc.sbuf_tensor("idb", [128, 128], BF16))
    idf = E(nc.sbuf_tensor("idf", [128, 128], F32))
    ones_row = E(nc.sbuf_tensor("ones_row", [1, 128], BF16))
    ones_f = E(nc.sbuf_tensor("ones_f", [1, 1], F32))
    eps_t = E(nc.sbuf_tensor("eps_t", [128, 1], F32))
    wn_col = E(nc.sbuf_tensor("wn_col", [128, 8], F32))
    bdt_col = E(nc.sbuf_tensor("bdt_col", [128, 16], F32))
    b1c = E(nc.sbuf_tensor("b1c", [128, 32, 3], F32))
    vrow = E(nc.sbuf_tensor("vrow", [1, D], BF16))
    gd0 = E(nc.sbuf_tensor("gd0", [128, 8], F32))
    gdiff = E(nc.sbuf_tensor("gdiff", [128, 8], BF16))
    carry = E(nc.sbuf_tensor("carry", [128, 16], F32))
    red = E(nc.sbuf_tensor("red", [128, 4], F32))
    sqs = E(nc.sbuf_tensor("sqs", [128, 4], F32))
    mu = E(nc.sbuf_tensor("mu", [128, 4], F32))
    mu2 = E(nc.sbuf_tensor("mu2", [128, 4], F32))
    var = E(nc.sbuf_tensor("var", [128, 4], F32))
    sd1 = E(nc.sbuf_tensor("sd1", [128, 4], F32))
    r1t = E(nc.sbuf_tensor("r1t", [128, 4], F32))
    t2t = E(nc.sbuf_tensor("t2t", [128, 4], F32))
    t3t = E(nc.sbuf_tensor("t3t", [128, 4], F32))
    sd2 = E(nc.sbuf_tensor("sd2", [128, 4], F32))
    r2t = E(nc.sbuf_tensor("r2t", [128, 4], F32))
    sig = E(nc.sbuf_tensor("sig", [128, 4], F32))
    s1t = E(nc.sbuf_tensor("s1t", [128, 4], F32))
    s2t = E(nc.sbuf_tensor("s2t", [128, 4], F32))
    se0 = E(nc.sbuf_tensor("se0", [1, TC], F32))
    mrow = E(nc.sbuf_tensor("mrow", [1, TC], BF16))
    mrep_sb = E(nc.sbuf_tensor("mrep_sb", [128, TC], BF16))
    mrep0_sb = E(nc.sbuf_tensor("mrep0_sb", [128, TC], BF16))

    _u_row = zs_sb[0:1, :, :].rearrange("p a t -> p (a t)")
    vacc = _u_row[:, 0:2048].bitcast(F32)
    vtmp = _u_row[:, 2048:4096].bitcast(F32)
    row_f = _u_row[:, 0:8192].bitcast(F32)
    s_v = U1[:, 0:8192].rearrange("p (i t) -> p i t", t=TC)
    hrelu_v = U1[:, 0:8192].bitcast(BF16).rearrange("p (h t) -> p h t", t=TC)
    lam_v = U2[:, 0:8192].rearrange("p (i t) -> p i t", t=TC)
    facc_v = U2[:, 0:4096].rearrange("p (s d) -> p s d", d=D)
    gTe_v = U2[:, 4096:6144].bitcast(BF16).rearrange("p (c t) -> p c t", t=TC)
    gTe_v2 = U2[:, 6144:8192].bitcast(BF16).rearrange("p (c t) -> p c t", t=TC)

    def hslot(sub):
        return u_sb[:, 2 * sub:2 * sub + 2, :].rearrange("p a t -> p (a t)")

    h4 = u_sb[:, 8:16, :].rearrange("p a t -> p (a t)").rearrange(
        "p (a d) -> p a d", d=D)

    pA = [E(nc.psum_tensor(f"pA{i}", [128, 512], F32)) for i in range(3)]
    pB = [E(nc.psum_tensor(f"pB{i}", [128, 512], F32)) for i in range(2)]
    pT = [E(nc.psum_tensor(f"pT{i}", [128, 1024], BF16)) for i in range(2)]
    pM = E(nc.psum_tensor("pM", [128, 512], F32))

    SW = Sem(E(nc.semaphore("SW")))
    SG = Sem(E(nc.semaphore("SG")))
    SO = Sem(E(nc.semaphore("SO")))
    PE = Sem(E(nc.semaphore("PE")))
    DV = Sem(E(nc.semaphore("DV")))
    AC = Sem(E(nc.semaphore("AC")))
    GP = Sem(E(nc.semaphore("GP")))

    block = E(nc.Block())
    S = Sched()

    ring_free = [None] * NRING
    ring_ptr = [0]

    def ring_slot(waits=()):
        slot = ring_ptr[0]
        ring_ptr[0] = (slot + 1) % NRING
        w = list(waits)
        if ring_free[slot] is not None:
            w.append(("CELL", ring_free[slot]))
        cell = [None]
        ring_free[slot] = cell
        return slot, w, cell

    def ring_load(img_ap, shape_free, waits=()):
        slot, w, cell = ring_slot(waits)
        t = S.add("sync", lambda e, s=slot, a=img_ap: e.dma_start(
            out=ring[:, s, :], in_=a), waits=w, inc=SW, dma=True)
        n = shape_free[0] * shape_free[1]
        view = ring[:, slot, 0:n].rearrange("p (a b) -> p a b", b=shape_free[1])
        return cell, view, t

    def ring_set_free(cell, sem, val):
        if cell[0] is None:
            cell[0] = []
        cell[0].append((sem, val))

    state = {
        "pA_free": [None, None, None], "pB_free": [None, None],
        "pT_free": [None, None],
        "out_t": 0,
        "prev_zs_reader": 0,
        "u_war_pe": 0,
        "h_pe": 0,
        "pe_f1_done": 0,
        "pe_c_mms": 0,
        "d_hT": (0, 0),
        "g_x": [0] * (NCHUNK + 1),
    }

    S.add("gpsimd", lambda e: e.memset(idb[:], 1.0))
    S.add("gpsimd", lambda e: e.affine_select(
        out=idb[:], in_=idb[:], compare_op=OP.is_equal, fill=0.0,
        base=0, pattern=[[-1, 128]], channel_multiplier=1))
    S.add("gpsimd", lambda e: e.memset(idf[:], 1.0))
    g_idf = S.add("gpsimd", lambda e: e.affine_select(
        out=idf[:], in_=idf[:], compare_op=OP.is_equal, fill=0.0,
        base=0, pattern=[[-1, 128]], channel_multiplier=1), inc=GP)
    S.add("gpsimd", lambda e: e.memset(ones_row[:], 1.0))
    S.add("gpsimd", lambda e: e.memset(ones_f[:], 1.0))
    g_id = S.add("gpsimd", lambda e: e.memset(eps_t[:], EPS), inc=GP)

    state["g_x"][0] = S.add("sync", lambda e: e.dma_start(
        out=x_sb[:], in_=x_in[0, 0:TC, :].rearrange("(s p) d -> p s d", p=128)),
        inc=SW, dma=True)

    def vec_to_cols(src_row_ap, n, dst_cols, pcol0, waits=(), ldq="sync"):
        nch = n // 128
        g = S.add(ldq, lambda e, a=src_row_ap, n=n: e.dma_start(
            out=row_f[:, 0:n], in_=a), waits=list(waits), inc=SW, dma=True)
        t = None
        for k in range(nch):
            t = S.add("tensor", lambda e, k=k: e.matmul(
                pM[:, pcol0 + k:pcol0 + k + 1], row_f[0:1, k * 128:(k + 1) * 128],
                ones_f[:], start=True, stop=True),
                waits=[(SW, g), (GP, g_id)] if k == 0 else (), inc=PE if k == nch - 1 else None)
        d = S.add("vector", lambda e, nch=nch: e.tensor_copy(
            out=dst_cols, in_=pM[:, pcol0:pcol0 + nch]),
            waits=[(PE, t)], inc=DV)
        return d

    d_wn = vec_to_cols(w_norm[None, :], D, wn_col[:], 0)

    g_v0 = S.add("sync", lambda e: e.dma_start(out=vacc[:], in_=b2[0][None, :]),
                 waits=[(DV, d_wn)], inc=SW, dma=True)
    g_v1 = S.add("sync", lambda e: e.dma_start(out=vtmp[:], in_=b2[1][None, :]),
                 inc=SW, dma=True)
    d_v1 = S.add("vector", lambda e: e.tensor_add(out=vacc[:], in0=vacc[:], in1=vtmp[:]),
                 waits=[(SW, g_v1)], inc=DV)
    g_v2 = S.add("sync", lambda e: e.dma_start(out=vtmp[:], in_=sb2[None, :]),
                 waits=[(DV, d_v1)], inc=SW, dma=True)
    S.add("vector", lambda e: e.tensor_add(out=vacc[:], in0=vacc[:], in1=vtmp[:]),
          waits=[(SW, g_v2)])
    d_vrow = S.add("vector", lambda e: e.tensor_copy(out=vrow[:], in_=vacc[:]), inc=DV)

    pool_a1 = [{"ap": x1_sb[:, 0:2, :], "free": None},
               {"ap": x1_sb[:, 2:4, :], "free": None},
               {"ap": U2[:, 0:2048].rearrange("p (a d) -> p a d", d=1024),
                "free": None},
               {"ap": U2[:, 2048:4096].rearrange("p (a d) -> p a d", d=1024),
                "free": None}]
    pool_a23 = [pool_a1[0], pool_a1[1],
                {"ap": tT_sb.rearrange("p a t -> p (a t)").bitcast(F32)
                 .rearrange("p (a d) -> p a d", d=1024), "free": None}]
    stage_rr = {"a1": 0, "a23": 0}
    bld = {"par": 0}

    def build_slab_c0(pool_name, pool, src_units, img_b, group_src, group_out,
                      store_ap):
        slot, wslot, cell = ring_slot()
        img_v = ring[:, slot, :].rearrange("p (a b) -> p a b", b=img_b)
        t_ld = [0, 0]
        units = [None, None]
        for ui in range(2):
            unit = pool[stage_rr[pool_name] % len(pool)]
            stage_rr[pool_name] += 1
            w = [unit["free"]] if unit["free"] is not None else []
            t_ld[ui] = S.add("sync", lambda e, o=unit["ap"], a=src_units[ui]:
                             e.dma_start(out=o, in_=a), waits=w, inc=SW, dma=True)
            units[ui] = unit
        ev_dv, ev_ac = 0, 0
        last_tp = [0, 0]
        for g in range(8):
            ui = g // 4
            par = bld["par"]
            bld["par"] ^= 1
            wt = [(SW, t_ld[ui]), (GP, g_idf)]
            if state["pB_free"][par] is not None:
                wt.append(state["pB_free"][par])
            srcs = group_src(g, units[ui]["ap"])
            tpe = None
            for k in range(4):
                tpe = S.add("tensor", lambda e, k=k, p=par, s=srcs[k]: e.transpose(
                    pB[p][:, k * 128:(k + 1) * 128], s, idf[:]),
                    waits=wt if k == 0 else (), inc=PE if k == 3 else None)
            last_tp[ui] = tpe
            wev = [(PE, tpe)] + (wslot if g <= 1 else [])
            outv = group_out(g, img_v)
            if par == 0:
                ev_dv = S.add("vector", lambda e, o=outv, p=par: e.tensor_copy(
                    out=o, in_=pB[p][:, 0:512].rearrange("p (k e) -> p k e", e=128)),
                    waits=wev, inc=DV)
                state["pB_free"][par] = (DV, ev_dv)
            else:
                ev_ac = S.add("scalar", lambda e, o=outv, p=par: e.copy(
                    out=o, in_=pB[p][:, 0:512].rearrange("p (k e) -> p k e", e=128)),
                    waits=wev, inc=AC)
                state["pB_free"][par] = (AC, ev_ac)
        for ui in range(2):
            units[ui]["free"] = (PE, last_tp[ui])
        mw = [(DV, ev_dv), (AC, ev_ac)]
        st = S.add("scalar", lambda e, s=slot, a=store_ap: e.dma_start(
            out=a.rearrange("p (a b) -> p a b", b=img_b),
            in_=ring[:, s, :].rearrange("p (a b) -> p a b", b=img_b)),
            waits=list(mw), inc=SW, dma=True)
        ring_set_free(cell, SW, st)
        return cell, img_v, mw, st

    def g_src_q(g, uv):
        return [uv[:, (g % 4) // 2,
                   ((g % 2) * 4 + k) * 128:((g % 2) * 4 + k + 1) * 128]
                for k in range(4)]

    def g_out_q(g, iv):
        col = ((g // 4) * 2 + (g % 4) // 2) * 128
        return iv[:, (g % 2) * 4:(g % 2) * 4 + 4, col:col + 128]

    def g_src_h(g, uv):
        fl = uv.rearrange("p a d -> p (a d)")
        return [fl[:, ((g % 4) * 4 + k) * 128:((g % 4) * 4 + k + 1) * 128]
                for k in range(4)]

    def g_out_h(g, iv):
        return iv[:, (g % 4) * 4:(g % 4) * 4 + 4,
                  (g // 4) * 128:(g // 4) * 128 + 128]

    def img_plain(img_slabs, nslab, shape, src_slice):
        last = None
        for k in range(nslab):
            last = S.add("gpsimd", lambda e, k=k, a=src_slice(k), b=shape[1]: e.dma_start(
                out=img_slabs[k].rearrange("p (a b) -> p a b", b=b), in_=a),
                inc=SG, dma=True)
        return last

    p_f1 = []
    p_f2 = []
    for p, (w1s, w2s) in enumerate([(sw1, sw2), (w1[0], w2[0]), (w1[1], w2[1])]):
        p_f1.append(img_plain(
            [t_f1[p][hs] for hs in range(8)], 8, (8, 512),
            lambda hs, s=w1s: s[:, hs * 512:(hs + 1) * 512]
            .rearrange("(dc q) h -> q dc h", q=128)))
        p_f2.append(img_plain(
            [t_f2[p][i] for i in range(8)], 8, (16, 256),
            lambda i, s=w2s: s[(i % 2) * 2048:(i % 2 + 1) * 2048,
                               (i // 2) * 256:(i // 2 + 1) * 256]
            .rearrange("(hc q) d -> q hc d", q=128)))

    def ln_first(s, tile, waits):
        S.add("vector", lambda e, s=s, t=tile: e.reduce_sum(
            out=red[:, s:s + 1], in_=t, axis=AX),
            waits=list(waits), drain=True)
        return S.add("scalar", lambda e, s=s, t=tile: e.activation(
            out=sq_scr[:], in_=t, func=AF.Square, accum_out=sqs[:, s:s + 1]),
            waits=list(waits), inc=AC)

    def ln_rest(n, rms, a_sq):
        d_var, a_sd1, d_t3, a_sd2 = [0] * n, [0] * n, [0] * n, [0] * n
        done = [0] * n
        for s in range(n):
            S.add("vector", lambda e, s=s: e.tensor_scalar_mul(
                out=mu[:, s:s + 1], in0=red[:, s:s + 1], scalar1=1.0 / D))
        for s in range(n):
            S.add("vector", lambda e, s=s: e.tensor_mul(
                out=mu2[:, s:s + 1], in0=mu[:, s:s + 1], in1=mu[:, s:s + 1]),
                waits=[(AC, a_sq[s])])
        for s in range(n):
            d_var[s] = S.add("vector", lambda e, s=s: e.scalar_tensor_tensor(
                out=var[:, s:s + 1], in0=sqs[:, s:s + 1], scalar=1.0 / D,
                in1=mu2[:, s:s + 1], op0=OP.mult, op1=OP.subtract),
                inc=DV, drain=(s == n - 1))
        for s in range(n):
            a_sd1[s] = S.add("scalar", lambda e, s=s: e.activation(
                out=sd1[:, s:s + 1], in_=var[:, s:s + 1], func=AF.Sqrt,
                bias=eps_t[:, 0:1]), waits=[(DV, d_var[s]), (GP, g_id)], inc=AC)
        d_rc = [0] * n
        for s in range(n):
            d_rc[s] = S.add("vector", lambda e, s=s: e.reciprocal(
                out=r1t[:, s:s + 1], in_=sd1[:, s:s + 1]),
                waits=[(AC, a_sd1[s])], inc=DV, drain=(s == n - 1))
        if not rms:
            return d_rc
        for s in range(n):
            S.add("vector", lambda e, s=s: e.tensor_mul(
                out=t2t[:, s:s + 1], in0=var[:, s:s + 1], in1=r1t[:, s:s + 1]))
        for s in range(n):
            d_t3[s] = S.add("vector", lambda e, s=s: e.tensor_mul(
                out=t3t[:, s:s + 1], in0=t2t[:, s:s + 1], in1=r1t[:, s:s + 1]),
                inc=DV, drain=(s == n - 1))
        for s in range(n):
            a_sd2[s] = S.add("scalar", lambda e, s=s: e.activation(
                out=sd2[:, s:s + 1], in_=t3t[:, s:s + 1], func=AF.Sqrt,
                bias=eps_t[:, 0:1]), waits=[(DV, d_t3[s])], inc=AC)
        for s in range(n):
            S.add("vector", lambda e, s=s: e.reciprocal(
                out=r2t[:, s:s + 1], in_=sd2[:, s:s + 1]),
                waits=[(AC, a_sd2[s])], drain=(s == n - 1))
        for s in range(n):
            S.add("vector", lambda e, s=s: e.tensor_mul(
                out=sig[:, s:s + 1], in0=r1t[:, s:s + 1], in1=r2t[:, s:s + 1]))
        for s in range(n):
            S.add("vector", lambda e, s=s: e.tensor_scalar_add(
                out=s1t[:, s:s + 1], in0=r1t[:, s:s + 1], scalar1=1.0))
        for s in range(n):
            done[s] = S.add("vector", lambda e, s=s: e.tensor_mul(
                out=s2t[:, s:s + 1], in0=mu[:, s:s + 1], in1=r1t[:, s:s + 1]),
                inc=DV, drain=(s == n - 1))
        return done

    def ln2_sub_chain(s, a_sq_s):
        S.add("vector", lambda e, s=s: e.tensor_scalar_mul(
            out=mu[:, s:s + 1], in0=red[:, s:s + 1], scalar1=1.0 / D),
            drain=True)
        S.add("vector", lambda e, s=s: e.tensor_mul(
            out=mu2[:, s:s + 1], in0=mu[:, s:s + 1], in1=mu[:, s:s + 1]),
            waits=[(AC, a_sq_s)], drain=True)
        d_var = S.add("vector", lambda e, s=s: e.scalar_tensor_tensor(
            out=var[:, s:s + 1], in0=sqs[:, s:s + 1], scalar=1.0 / D,
            in1=mu2[:, s:s + 1], op0=OP.mult, op1=OP.subtract),
            inc=DV, drain=True)
        a_sd1 = S.add("scalar", lambda e, s=s: e.activation(
            out=sd1[:, s:s + 1], in_=var[:, s:s + 1], func=AF.Sqrt,
            bias=eps_t[:, 0:1]), waits=[(DV, d_var), (GP, g_id)], inc=AC)
        return S.add("vector", lambda e, s=s: e.reciprocal(
            out=r1t[:, s:s + 1], in_=sd1[:, s:s + 1]),
            waits=[(AC, a_sd1)], inc=DV, drain=True)

    def transposes_to_tT(src, sub, wait_dve, scale_cols, wait_ac=None):
        par = sub % 2
        w = [(GP, g_id), (PE, state["h_pe"])]
        if wait_dve is not None:
            w.append((DV, wait_dve))
        if wait_ac is not None:
            w.append((AC, wait_ac))
        if state["pT_free"][par] is not None:
            w += state["pT_free"][par]
        t = None
        for dc in range(8):
            t = S.add("tensor", lambda e, d=dc, p=par, s=src: e.transpose(
                pT[p][:, d * 128:(d + 1) * 128], s[:, d * 128:(d + 1) * 128], idb[:]),
                waits=w if dc == 0 else (), inc=PE if dc == 7 else None)
        state["h_pe"] = t
        eng, SEM = ("vector", DV) if sub % 2 == 0 else ("scalar", AC)
        if scale_cols is None:
            if eng == "vector":
                d_ev = S.add(eng, lambda e, p=par: e.tensor_copy(
                    out=tT_sb[:, :, sub * 128:(sub + 1) * 128],
                    in_=pT[p].rearrange("p (a b) -> p a b", b=128)),
                    waits=[(PE, t)], inc=SEM)
            else:
                d_ev = S.add(eng, lambda e, p=par: e.copy(
                    out=tT_sb[:, :, sub * 128:(sub + 1) * 128],
                    in_=pT[p].rearrange("p (a b) -> p a b", b=128)),
                    waits=[(PE, t)], inc=SEM)
        else:
            for dc in range(8):
                if eng == "vector":
                    d_ev = S.add(eng, lambda e, d=dc, p=par: e.tensor_scalar_mul(
                        out=tT_sb[:, d, sub * 128:(sub + 1) * 128],
                        in0=pT[p][:, d * 128:(d + 1) * 128],
                        scalar1=scale_cols[:, d:d + 1]),
                        waits=[(PE, t)] if dc == 0 else (), inc=SEM if dc == 7 else None)
                else:
                    d_ev = S.add(eng, lambda e, d=dc, p=par: e.activation(
                        out=tT_sb[:, d, sub * 128:(sub + 1) * 128],
                        in_=pT[p][:, d * 128:(d + 1) * 128],
                        func=AF.Copy, scale=scale_cols[:, d:d + 1]),
                        waits=[(PE, t)] if dc == 0 else (), inc=SEM if dc == 7 else None)
        evs = [(SEM, d_ev)]
        state["pT_free"][par] = evs
        return evs

    def emit_ln1(c):
        fw = [(SW, state["g_x"][c])]
        a_sq = [ln_first(s, x_sb[:, s, :], fw) for s in range(NSUB)]
        done = ln_rest(NSUB, True, a_sq)
        d_hs = []
        for s in range(NSUB):
            wh = [(DV, done[s])]
            d_h = S.add("vector", lambda e, s=s: e.tensor_scalar(
                out=hslot(s)[:], in0=x_sb[:, s, :], scalar1=mu[:, s:s + 1],
                scalar2=sig[:, s:s + 1], op0=OP.subtract, op1=OP.mult),
                waits=wh, inc=DV)
            S.add("vector", lambda e, s=s: e.tensor_scalar(
                out=x_sb[:, s, :], in0=x_sb[:, s, :], scalar1=s1t[:, s:s + 1],
                scalar2=s2t[:, s:s + 1], op0=OP.mult, op1=OP.subtract))
            d_hs.append(d_h)
        return d_hs

    def emit_ln1_transposes(d_hs):
        dvm, acm = 0, 0
        for sub in range(NSUB):
            for sem, val in transposes_to_tT(hslot(sub), sub, d_hs[sub],
                                             scale_cols=wn_col):
                if sem is DV:
                    dvm = max(dvm, val)
                else:
                    acm = max(acm, val)
        return [(DV, dvm), (AC, acm)]

    d_hs0 = emit_ln1(0)
    state["d_hT"] = emit_ln1_transposes(d_hs0)

    d_bdt = vec_to_cols(b_dt[None, :], INNER, bdt_col[:], 16, waits=[(DV, d_vrow)])
    d_b1 = 0
    for p, src in enumerate([sb1, b1[0], b1[1]]):
        d_b1 = vec_to_cols(src[None, :], HID, b1c[:, :, p], 48 + 32 * p,
                           waits=[(DV, d_bdt if p == 0 else d_b1)])
    d_g0 = vec_to_cols(gate_w[0][None, :], D, gd0[:], 144, waits=[(DV, d_b1)])
    d_g1 = vec_to_cols(gate_w[1][None, :], D, gd0[:], 152, waits=[(DV, d_g0)])
    d_gd = S.add("vector", lambda e: e.tensor_tensor(
        out=gdiff[:], in0=gd0[:], in1=pM[:, 144:152], op=OP.subtract),
        waits=[(DV, d_g1)], inc=DV)
    state["u_war_pe"] = PE.n

    for c in range(NCHUNK):
        b = c // KCH
        kc = c % KCH
        k0 = kc * TC

        d_hT = state["d_hT"]

        act_uz_wait = [(PE, max(state["u_war_pe"], state["prev_zs_reader"])),
                       (DV, d_gd)]
        first_uz = True
        for es in range(8):
            if c == 0:
                srcs = [W_in[es * 512 + ui * 256:es * 512 + (ui + 1) * 256, :]
                        .rearrange("(a q) d -> q a d", q=128) for ui in range(2)]
                slot_c, wview, mw, st = build_slab_c0(
                    "a1", pool_a1, srcs, 512, g_src_q, g_out_q, t_a1[es])
                state["st_a1"] = st
                wbase = mw
            else:
                wld = [(SW, state["st_a1"])] if c == 1 else []
                slot_c, wview, t_w = ring_load(t_a1[es], (8, 512), waits=wld)
                wbase = [(SW, t_w)]
            last_pe = None
            for el in range(4):
                ec = es * 4 + el
                par = ec % 3
                w = wbase + d_hT
                if state["pA_free"][par] is not None:
                    w.append(state["pA_free"][par])
                for dc in range(8):
                    t_mm = S.add("tensor", lambda e, d=dc, p=par, v=wview, el=el: e.matmul(
                        pA[p][:], v[:, d, el * 128:(el + 1) * 128],
                        tT_sb[:, d, :], start=(d == 0), stop=(d == 7)),
                        waits=w if dc == 0 else (), inc=PE if dc == 7 else None)
                last_pe = t_mm
                tgt = (u_sb, ec) if ec < 16 else (zs_sb, ec - 16)
                d_ev = S.add("scalar", lambda e, p=par, t=tgt: e.activation(
                    out=t[0][:, t[1], :], in_=pA[p][:], func=AF.Silu),
                    waits=[(PE, t_mm)] + (act_uz_wait if first_uz else []), inc=AC)
                first_uz = False
                state["pA_free"][par] = (AC, d_ev)
            ring_set_free(slot_c, PE, last_pe)
        a_uz_all = AC.n

        if c == 0:
            pool_a23[2]["free"] = (PE, PE.n)

        a_lam = []
        bp_emitted = [0]
        scan_emitted = [0]
        d_bp = [0] * 16
        d_m_l = [0] * 16

        def emit_bp(upto, a_lam=a_lam, bp_emitted=bp_emitted, d_bp=d_bp):
            while bp_emitted[0] < min(upto, 16):
                ic = bp_emitted[0]
                d_bp[ic] = S.add("vector", lambda e, i=ic: e.scalar_tensor_tensor(
                    out=b_scr2[i % 2][:],
                    in0=lam_v[:, i, :], scalar=1.0, in1=u_sb[:, i, :],
                    op0=OP.subtract, op1=OP.mult),
                    waits=[(AC, a_lam[ic])], inc=DV)
                bp_emitted[0] += 1

        def emit_scan_ops(upto, kc=kc, scan_emitted=scan_emitted, d_bp=d_bp,
                          d_m_l=d_m_l):
            emit_bp(upto + 1)
            while scan_emitted[0] < upto:
                ic = scan_emitted[0]
                init = 0.0 if kc == 0 else carry[:, ic:ic + 1]
                S.add("vector", lambda e, i=ic, init=init: e.tensor_tensor_scan(
                    out=s_v[:, i, :], data0=lam_v[:, i, :],
                    data1=b_scr2[i % 2][:],
                    initial=init, op0=OP.mult, op1=OP.add),
                    waits=[(DV, d_bp[ic])], drain=True)
                d_m_l[ic] = S.add("vector", lambda e, i=ic: e.tensor_mul(
                    out=zs_sb[:, i, :], in0=s_v[:, i, :], in1=zs_sb[:, i, :]),
                    inc=DV)
                scan_emitted[0] += 1

        for js in range(8):
            if c == 0:
                srcs = [W_dt[js * 256 + jl * 128:js * 256 + (jl + 1) * 128, :]
                        .rearrange("q (a d) -> q a d", d=1024) for jl in range(2)]
                slot_c, wview, mw, st = build_slab_c0(
                    "a23", pool_a23, srcs, 256, g_src_h, g_out_h, t_a2[js])
                state["st_a2"] = st
                wbase = mw
            else:
                wld = [(SW, state["st_a2"])] if c == 1 else []
                slot_c, wview, t_w = ring_load(t_a2[js], (16, 256), waits=wld)
                wbase = [(SW, t_w)]
            last_pe = None
            for jl in range(2):
                j = js * 2 + jl
                par = j % 3
                w = wbase + [(AC, a_uz_all)]
                if state["pA_free"][par] is not None:
                    w.append(state["pA_free"][par])
                for ic in range(16):
                    t_mm = S.add("tensor", lambda e, i=ic, p=par, v=wview, jl=jl: e.matmul(
                        pA[p][:], v[:, i, jl * 128:(jl + 1) * 128],
                        u_sb[:, i, :], start=(i == 0), stop=(i == 15)),
                        waits=w if ic == 0 else (), inc=PE if ic == 15 else None)
                last_pe = t_mm
                wlam = [(PE, t_mm)]
                if c > 0:
                    wlam.append((SO, state["out_t"]))
                a_ev = S.add("scalar", lambda e, p=par, j=j: e.activation(
                    out=lam_v[:, j, :], in_=pA[p][:], func=AF.Sigmoid,
                    bias=bdt_col[:, j:j + 1]), waits=wlam, inc=AC)
                a_lam.append(a_ev)
                state["pA_free"][par] = (AC, a_ev)
                emit_scan_ops(len(a_lam) - 1)
            ring_set_free(slot_c, PE, last_pe)
        pe_a2_all = PE.n
        emit_scan_ops(16)
        d_m_all = DV.n
        for ic in range(16):
            S.add("vector", lambda e, i=ic: e.tensor_copy(
                out=carry[:, i:i + 1], in_=s_v[:, i, TC - 1:TC]))

        S.add("scalar", lambda e: e.activation(
            out=sd1[:, 0:1], in_=eps_t[:, 0:1], func=AF.Sqrt),
            waits=[(GP, g_id)])
        if c == 0:
            slabs = []
            dvm, acm = 0, 0
            for ks in range(4):
                dh, ih = ks // 2, ks % 2
                srcs = [W_out[dh * 512 + up * 256:dh * 512 + (up + 1) * 256,
                              ih * 1024:(ih + 1) * 1024]
                        .rearrange("(a q) i -> q a i", q=128) for up in range(2)]
                cell, iv, mw, st = build_slab_c0(
                    "a23", pool_a23, srcs, 512, g_src_q, g_out_q, t_c[ks])
                state["st_c"] = st
                dvm, acm = max(dvm, mw[0][1]), max(acm, mw[1][1])
                slabs.append((cell, iv, None))
            wC = [(DV, dvm), (AC, acm)]
        else:
            wld = [(SW, state["st_c"])] if c == 1 else []
            slabs = [ring_load(t_c[k], (8, 512), waits=wld) for k in range(4)]
            wC = [(SW, slabs[3][2])]
        first_cg = True
        d_g2 = [0] * NSUB
        for sub in range(NSUB):
            for dh in range(2):
                par = (2 * sub + dh) % 2
                w = wC + ([] if first_cg else [(DV, d_m_all)])
                if state["pB_free"][par] is not None:
                    w.append(state["pB_free"][par])
                t_mm = None
                for ih in range(2):
                    v = slabs[2 * dh + ih][1]
                    for il in range(8):
                        ic = ih * 8 + il
                        mmw = list(w) if ic == 0 else []
                        if first_cg:
                            mmw.append((DV, d_m_l[ic]))
                        t_mm = S.add("tensor",
                                     lambda e, i=ic, il=il, v=v, p=par, sub=sub,
                                     first=(ic == 0), last=(ic == 15):
                                     e.matmul(pB[p][:],
                                              zs_sb[:, i, sub * 128:(sub + 1) * 128],
                                              v[:, il, :], start=first, stop=last),
                                     waits=mmw, inc=PE if ic == 15 else None)
                first_cg = False
                state["pe_c_mms"] = t_mm
                d_x1 = S.add("vector", lambda e, p=par, sub=sub, dh=dh: e.tensor_tensor(
                    out=x1_sb[:, sub, dh * 512:(dh + 1) * 512],
                    in0=x_sb[:, sub, dh * 512:(dh + 1) * 512],
                    in1=pB[p][:], op=OP.subtract),
                    waits=[(PE, t_mm)], inc=DV)
                state["pB_free"][par] = (DV, d_x1)
                if dh == 1:
                    state[f"d_x1_{sub}"] = d_x1
                    state["d_x1_last"] = d_x1
                    a_sq_s = ln_first(sub, x1_sb[:, sub, :], [(DV, d_x1)])
                    if sub < NSUB - 1:
                        d_g2[sub] = ln2_sub_chain(sub, a_sq_s)
                    else:
                        sq_last = a_sq_s
        for cell, _, _ in slabs:
            ring_set_free(cell, PE, state["pe_c_mms"])
        state["prev_zs_reader"] = state["pe_c_mms"]

        d_gv = [0] * NSUB
        for sub in range(NSUB - 1):
            d_gv[sub] = S.add("vector", lambda e, s=sub: e.tensor_scalar(
                out=h4[:, s, :], in0=x1_sb[:, s, :], scalar1=mu[:, s:s + 1],
                scalar2=r1t[:, s:s + 1], op0=OP.subtract, op1=OP.mult),
                waits=[(DV, d_g2[sub]), (PE, state["h_pe"])], inc=DV)
        ls = NSUB - 1
        d_g2[ls] = ln2_sub_chain(ls, sq_last)
        d_gv[ls] = S.add("vector", lambda e, s=ls: e.tensor_scalar(
            out=h4[:, s, :], in0=x1_sb[:, s, :], scalar1=mu[:, s:s + 1],
            scalar2=r1t[:, s:s + 1], op0=OP.subtract, op1=OP.mult),
            waits=[(DV, d_g2[ls]), (PE, state["h_pe"])], inc=DV)
        dvm, acm = 0, 0
        for sub in range(NSUB):
            for sem, val in transposes_to_tT(h4[:, sub, :], sub, d_gv[sub],
                                             scale_cols=None):
                if sem is DV:
                    dvm = max(dvm, val)
                else:
                    acm = max(acm, val)
        d_gT = [(DV, dvm), (AC, acm)]

        d_hs_next = None
        if c + 1 < NCHUNK:
            b2_, kc2 = (c + 1) // KCH, (c + 1) % KCH
            state["g_x"][c + 1] = S.add("sync", lambda e, b=b2_, k0=kc2 * TC: e.dma_start(
                out=x_sb[:], in_=x_in[b, k0:k0 + TC, :].rearrange("(s p) d -> p s d", p=128)),
                waits=[(DV, state["d_x1_last"])], inc=SW, dma=True)
            d_hs_next = emit_ln1(c + 1)

        t_sd = None
        for dc in range(8):
            t_sd = S.add("tensor", lambda e, d=dc: e.matmul(
                pM[0:1, :], gdiff[:, d:d + 1], tT_sb[:, d, :],
                start=(d == 0), stop=(d == 7)),
                waits=d_gT + [(DV, d_gd)] if dc == 0 else (),
                inc=PE if dc == 7 else None)

        def emit_mask_section():
            S.add("vector", lambda e: e.tensor_single_scalar(
                out=se0[:], in_=pM[0:1, :], scalar=0.0, op=OP.is_gt),
                waits=[(PE, t_sd)])
            d_mrow = S.add("vector", lambda e: e.tensor_copy(out=mrow[:], in_=se0[:]), inc=DV)
            t_mrep = S.add("tensor", lambda e: e.matmul(
                pM[:, :], ones_row[:], mrow[:], start=True, stop=True),
                waits=[(DV, d_mrow)], inc=PE)
            d_mrep = S.add("vector", lambda e: e.tensor_copy(out=mrep_sb[:], in_=pM[:]),
                           waits=[(PE, t_mrep)], inc=DV)
            d_mrep0 = S.add("vector", lambda e: e.tensor_scalar(
                out=mrep0_sb[:], in0=mrep_sb[:], scalar1=-1.0, scalar2=1.0,
                op0=OP.mult, op1=OP.add), inc=DV)
            return d_mrep, d_mrep0

        pe_f2_prev = 0
        for p in range(3):
            if p == 0:
                rhs_view, d_rhs = tT_sb, list(d_gT)
                d_rhs_dc = None
            else:
                msk = mrep0_sb if p == 1 else mrep_sb
                d_msk = d_mrep0 if p == 1 else d_mrep
                d_rhs_dc = []
                for dc in range(8):
                    wm = [(DV, d_msk)]
                    if p == 2:
                        wm.append((PE, state["pe_f1_done"]))
                    d_b = S.add("vector", lambda e, d=dc, m=msk: e.tensor_mul(
                        out=gTe_v[:, d, :], in0=tT_sb[:, d, :], in1=m[:]),
                        waits=wm if dc == 0 else (), inc=DV)
                    d_rhs_dc.append(d_b)
                d_rhs = [(DV, d_rhs_dc[7])]
                rhs_view = gTe_v
            sl2_next = []
            first_f1 = True
            for hs in range(8):
                wslab = [(SG, p_f1[p])] if c == 0 else []
                slot_c, wview, t_w = ring_load(t_f1[p][hs], (8, 512), waits=wslab)
                if hs in (4, 5):
                    sl2_next.append(ring_load(
                        t_f2[p][hs - 4], (16, 256),
                        waits=[(SG, p_f2[p])] if c == 0 else ()))
                last_pe = None
                for hl in range(4):
                    hc = hs * 4 + hl
                    par = hc % 3
                    per_dc = d_rhs_dc is not None and hs == 0 and hl == 0
                    w = [(SW, t_w)] + ([] if per_dc else d_rhs)
                    if first_f1:
                        w.append((PE, pe_f2_prev))
                        first_f1 = False
                    if state["pA_free"][par] is not None:
                        w.append(state["pA_free"][par])
                    for dc in range(8):
                        mmw = list(w) if dc == 0 else []
                        if per_dc:
                            mmw.append((DV, d_rhs_dc[dc]))
                        t_mm = S.add("tensor", lambda e, d=dc, p2=par, v=wview, hl=hl, rv=rhs_view:
                                     e.matmul(pA[p2][:], v[:, d, hl * 128:(hl + 1) * 128],
                                              rv[:, d, :], start=(d == 0), stop=(d == 7)),
                                     waits=mmw, inc=PE if dc == 7 else None)
                    last_pe = t_mm
                    d_ev = S.add("scalar", lambda e, p2=par, hc=hc, p=p: e.activation(
                        out=hrelu_v[:, hc, :], in_=pA[p2][:], func=AF.Relu,
                        bias=b1c[:, hc:hc + 1, p]), waits=[(PE, t_mm)], inc=AC)
                    state["pA_free"][par] = (AC, d_ev)
                ring_set_free(slot_c, PE, last_pe)
            a_f1_all = AC.n
            state["pe_f1_done"] = PE.n
            if p == 0:
                d_mrep, d_mrep0 = emit_mask_section()

            if p == 2 and d_hs_next is not None:
                state["d_hT"] = emit_ln1_transposes(d_hs_next)
                state["u_war_pe"] = PE.n

            for dq in range(4):
                sl2 = sl2_next
                sl2_next = []
                if dq < 3:
                    sl2_next = [
                        ring_load(t_f2[p][(dq + 1) * 2 + hh], (16, 256),
                                  waits=[(SG, p_f2[p])] if c == 0 else ())
                        for hh in range(2)]
                for sub in range(NSUB):
                    par = sub % 2
                    w = [(SW, sl2[1][2]), (AC, a_f1_all)]
                    if p == 0:
                        w.append((DV, d_vrow))
                    if state["pB_free"][par] is not None:
                        w.append(state["pB_free"][par])
                    t_mm = None
                    nmm = 0
                    if p == 0:
                        t_mm = S.add("tensor", lambda e, p2=par, dq=dq: e.matmul(
                            pB[p2][:, 0:256], ones_row[:], vrow[0:1, dq * 256:(dq + 1) * 256],
                            start=True, stop=False), waits=w)
                        nmm = 1
                    for hh in range(2):
                        v = sl2[hh][1]
                        for hl in range(16):
                            hc = hh * 16 + hl
                            first = nmm == 0 and hc == 0
                            last = hc == 31
                            t_mm = S.add("tensor",
                                         lambda e, p2=par, hc=hc, hl=hl, v=v, sub=sub,
                                         first=first, last=last:
                                         e.matmul(pB[p2][:, 0:256],
                                                  hrelu_v[:, hc, sub * 128:(sub + 1) * 128],
                                                  v[:, hl, :], start=first, stop=last),
                                         waits=w if first else (), inc=PE if last else None)
                    if p == 0:
                        d_ev = S.add("vector", lambda e, p2=par, sub=sub, dq=dq: e.tensor_tensor(
                            out=facc_v[:, sub, dq * 256:(dq + 1) * 256],
                            in0=x1_sb[:, sub, dq * 256:(dq + 1) * 256],
                            in1=pB[p2][:, 0:256], op=OP.add),
                            waits=[(PE, t_mm)], inc=DV)
                    else:
                        d_ev = S.add("vector", lambda e, p2=par, sub=sub, dq=dq: e.tensor_tensor(
                            out=facc_v[:, sub, dq * 256:(dq + 1) * 256],
                            in0=facc_v[:, sub, dq * 256:(dq + 1) * 256],
                            in1=pB[p2][:, 0:256], op=OP.add),
                            waits=[(PE, t_mm)], inc=DV)
                    state["pB_free"][par] = (DV, d_ev)
                for cell, _, _ in sl2:
                    ring_set_free(cell, PE, PE.n)
            pe_f2_prev = PE.n
        d_facc = DV.n

        state["out_t"] = S.add("gpsimd", lambda e, b=b, k0=k0: e.dma_start(
            out=y_out[b, k0:k0 + TC, :].rearrange("(s p) d -> p s d", p=128),
            in_=facc_v[:]),
            waits=[(DV, d_facc)], inc=SO, dma=True)

    S.add("gpsimd", lambda e: None, waits=[(SO, SO.n)])

    @block.sync
    def _(eng):
        S.replay("sync", eng)

    @block.tensor
    def _(eng):
        S.replay("tensor", eng)

    @block.vector
    def _(eng):
        S.replay("vector", eng)

    @block.scalar
    def _(eng):
        S.replay("scalar", eng)

    @block.gpsimd
    def _(eng):
        S.replay("gpsimd", eng)

    stack.close()
    return nc


_NC_CACHE = None


def kernel(**inputs):
    global _NC_CACHE
    if _NC_CACHE is None:
        _NC_CACHE = build_program()
    nc = _NC_CACHE

    x = np.ascontiguousarray(inputs["L0_attn_x"])
    shards = np.split(x, NCORES, axis=0)
    base = {k: np.ascontiguousarray(v) for k, v in inputs.items() if k != "L0_attn_x"}
    in_maps = []
    for c in range(NCORES):
        m = dict(base)
        m["L0_attn_x"] = shards[c]
        in_maps.append(m)
    res = run_bass_kernel_spmd(nc, in_maps, core_ids=list(range(NCORES)))
    out = np.concatenate([np.asarray(r["y"]) for r in res.results], axis=0)
    return out.astype(np.float32)


# version 42
# speedup vs baseline: 1.0521x; 1.0521x over previous
import numpy as np
import concourse.bass as bass
import concourse.mybir as mybir
from concourse.bass_utils import run_bass_kernel_spmd

F32 = mybir.dt.float32
BF16 = mybir.dt.bfloat16
AF = mybir.ActivationFunctionType
OP = mybir.AluOpType
AX = mybir.AxisListType.X

NCORES = 8
BPC = 4
SEQ = 2048
D = 1024
INNER = 2048
HID = 4096
TC = 512
NSUB = TC // 128
KCH = SEQ // TC
NCHUNK = BPC * KCH
EPS = 1e-5
NRING = 7

ENGS = ("sync", "tensor", "vector", "scalar", "gpsimd")


class Sem:
    def __init__(self, handle):
        self.h = handle
        self.n = 0


class Sched:
    def __init__(self):
        self.ops = {e: [] for e in ENGS}

    def add(self, eng, fn, waits=(), inc=None, dma=False, drain=False):
        amt = 16 if dma else 1
        incs = []
        if inc is not None:
            for s in (inc if isinstance(inc, (list, tuple)) else [inc]):
                s.n += amt
                incs.append((s, amt))
        self.ops[eng].append((fn, tuple(waits), tuple(incs), drain))
        return incs[0][0].n if incs else None

    def replay(self, eng_name, eng):
        floor = {}
        for fn, waits, incs, drain in self.ops[eng_name]:
            for s, v in waits:
                if s == "CELL":
                    assert v[0] is not None, "unresolved ring free cell"
                    pairs = v[0]
                else:
                    pairs = [(s, v)]
                for s2, v2 in pairs:
                    if floor.get(id(s2), 0) < v2:
                        eng.wait_ge(s2.h, v2)
                        floor[id(s2)] = v2
            inst = fn(eng)
            for s, amt in incs:
                inst.then_inc(s.h, amt)
            if drain:
                eng.drain()


def build_program():
    nc = bass.Bass("TRN2", target_bir_lowering=False, debug=False)

    x_in = nc.dram_tensor("L0_attn_x", [BPC, SEQ, D], F32, kind="ExternalInput").ap()
    w_norm = nc.dram_tensor("L0_attn_w_norm", [D], F32, kind="ExternalInput").ap()
    W_in = nc.dram_tensor("L0_attn_W_in", [2 * INNER, D], F32, kind="ExternalInput").ap()
    W_dt = nc.dram_tensor("L0_attn_W_dt", [INNER, INNER], F32, kind="ExternalInput").ap()
    b_dt = nc.dram_tensor("L0_attn_b_dt", [INNER], F32, kind="ExternalInput").ap()
    W_out = nc.dram_tensor("L0_attn_W_out", [D, INNER], F32, kind="ExternalInput").ap()
    gate_w = nc.dram_tensor("L0_ffn_gate_w", [2, D], F32, kind="ExternalInput").ap()
    w1 = nc.dram_tensor("L0_ffn_w1", [2, D, HID], F32, kind="ExternalInput").ap()
    b1 = nc.dram_tensor("L0_ffn_b1", [2, HID], F32, kind="ExternalInput").ap()
    w2 = nc.dram_tensor("L0_ffn_w2", [2, HID, D], F32, kind="ExternalInput").ap()
    b2 = nc.dram_tensor("L0_ffn_b2", [2, D], F32, kind="ExternalInput").ap()
    sw1 = nc.dram_tensor("L0_ffn_sw1", [D, HID], F32, kind="ExternalInput").ap()
    sb1 = nc.dram_tensor("L0_ffn_sb1", [HID], F32, kind="ExternalInput").ap()
    sw2 = nc.dram_tensor("L0_ffn_sw2", [HID, D], F32, kind="ExternalInput").ap()
    sb2 = nc.dram_tensor("L0_ffn_sb2", [D], F32, kind="ExternalInput").ap()
    y_out = nc.dram_tensor("y", [BPC, SEQ, D], F32, kind="ExternalOutput").ap()

    t_a1 = nc.dram_tensor("t_a1", [8, 128, 4096], BF16, kind="Internal").ap()
    t_a2 = nc.dram_tensor("t_a2", [8, 128, 4096], BF16, kind="Internal").ap()
    t_c = nc.dram_tensor("t_c", [4, 128, 4096], BF16, kind="Internal").ap()
    t_f1 = nc.dram_tensor("t_f1", [3, 8, 128, 4096], BF16, kind="Internal").ap()
    t_f2 = nc.dram_tensor("t_f2", [3, 8, 128, 4096], BF16, kind="Internal").ap()

    from contextlib import ExitStack
    stack = ExitStack()
    E = stack.enter_context

    x_sb = E(nc.sbuf_tensor("x_sb", [128, NSUB, D], F32))
    x1_sb = E(nc.sbuf_tensor("x1_sb", [128, NSUB, D], F32))
    tT_sb = E(nc.sbuf_tensor("tT_sb", [128, 8, TC], BF16))
    u_sb = E(nc.sbuf_tensor("u_sb", [128, 16, TC], BF16))
    zs_sb = E(nc.sbuf_tensor("zs_sb", [128, 16, TC], BF16))
    U1 = E(nc.sbuf_tensor("U1", [128, 8192], F32))
    U2 = E(nc.sbuf_tensor("U2", [128, 8192], F32))
    ring = E(nc.sbuf_tensor("ring", [128, NRING, 4096], BF16))
    b_scr = E(nc.sbuf_tensor("b_scr", [128, 2, TC], BF16))
    b_scr2 = [b_scr[:, 0, :], b_scr[:, 1, :]]
    sq_scr = E(nc.sbuf_tensor("sq_scr", [128, D], BF16))
    idb = E(nc.sbuf_tensor("idb", [128, 128], BF16))
    idf = E(nc.sbuf_tensor("idf", [128, 128], F32))
    ones_row = E(nc.sbuf_tensor("ones_row", [1, 128], BF16))
    ones_f = E(nc.sbuf_tensor("ones_f", [1, 1], F32))
    eps_t = E(nc.sbuf_tensor("eps_t", [128, 1], F32))
    wn_col = E(nc.sbuf_tensor("wn_col", [128, 8], F32))
    bdt_col = E(nc.sbuf_tensor("bdt_col", [128, 16], F32))
    b1c = E(nc.sbuf_tensor("b1c", [128, 32, 3], F32))
    vrow = E(nc.sbuf_tensor("vrow", [1, D], BF16))
    gd0 = E(nc.sbuf_tensor("gd0", [128, 8], F32))
    gdiff = E(nc.sbuf_tensor("gdiff", [128, 8], BF16))
    carry = E(nc.sbuf_tensor("carry", [128, 16], F32))
    red = E(nc.sbuf_tensor("red", [128, 4], F32))
    sqs = E(nc.sbuf_tensor("sqs", [128, 4], F32))
    mu = E(nc.sbuf_tensor("mu", [128, 4], F32))
    mu2 = E(nc.sbuf_tensor("mu2", [128, 4], F32))
    var = E(nc.sbuf_tensor("var", [128, 4], F32))
    sd1 = E(nc.sbuf_tensor("sd1", [128, 4], F32))
    r1t = E(nc.sbuf_tensor("r1t", [128, 4], F32))
    t2t = E(nc.sbuf_tensor("t2t", [128, 4], F32))
    t3t = E(nc.sbuf_tensor("t3t", [128, 4], F32))
    sd2 = E(nc.sbuf_tensor("sd2", [128, 4], F32))
    r2t = E(nc.sbuf_tensor("r2t", [128, 4], F32))
    sig = E(nc.sbuf_tensor("sig", [128, 4], F32))
    s1t = E(nc.sbuf_tensor("s1t", [128, 4], F32))
    s2t = E(nc.sbuf_tensor("s2t", [128, 4], F32))
    se0 = E(nc.sbuf_tensor("se0", [1, TC], F32))
    mrow = E(nc.sbuf_tensor("mrow", [1, TC], BF16))
    mrep_sb = E(nc.sbuf_tensor("mrep_sb", [128, TC], BF16))
    mrep0_sb = E(nc.sbuf_tensor("mrep0_sb", [128, TC], BF16))

    _u_row = zs_sb[0:1, :, :].rearrange("p a t -> p (a t)")
    vacc = _u_row[:, 0:2048].bitcast(F32)
    vtmp = _u_row[:, 2048:4096].bitcast(F32)
    row_f = _u_row[:, 0:8192].bitcast(F32)
    s_v = U1[:, 0:8192].rearrange("p (i t) -> p i t", t=TC)
    hrelu_v = U1[:, 0:8192].bitcast(BF16).rearrange("p (h t) -> p h t", t=TC)
    lam_v = U2[:, 0:8192].rearrange("p (i t) -> p i t", t=TC)
    facc_v = U2[:, 0:4096].rearrange("p (s d) -> p s d", d=D)
    gTe_v = U2[:, 4096:6144].bitcast(BF16).rearrange("p (c t) -> p c t", t=TC)
    gTe_v2 = U2[:, 6144:8192].bitcast(BF16).rearrange("p (c t) -> p c t", t=TC)

    def hslot(sub):
        return u_sb[:, 2 * sub:2 * sub + 2, :].rearrange("p a t -> p (a t)")

    h4 = u_sb[:, 8:16, :].rearrange("p a t -> p (a t)").rearrange(
        "p (a d) -> p a d", d=D)

    pA = [E(nc.psum_tensor(f"pA{i}", [128, 512], F32)) for i in range(3)]
    pB = [E(nc.psum_tensor(f"pB{i}", [128, 512], F32)) for i in range(2)]
    pT = [E(nc.psum_tensor(f"pT{i}", [128, 1024], BF16)) for i in range(2)]
    pM = E(nc.psum_tensor("pM", [128, 512], F32))

    SW = Sem(E(nc.semaphore("SW")))
    SG = Sem(E(nc.semaphore("SG")))
    SO = Sem(E(nc.semaphore("SO")))
    PE = Sem(E(nc.semaphore("PE")))
    DV = Sem(E(nc.semaphore("DV")))
    AC = Sem(E(nc.semaphore("AC")))
    GP = Sem(E(nc.semaphore("GP")))

    block = E(nc.Block())
    S = Sched()

    ring_free = [None] * NRING
    ring_ptr = [0]

    def ring_slot(waits=()):
        slot = ring_ptr[0]
        ring_ptr[0] = (slot + 1) % NRING
        w = list(waits)
        if ring_free[slot] is not None:
            w.append(("CELL", ring_free[slot]))
        cell = [None]
        ring_free[slot] = cell
        return slot, w, cell

    def ring_load(img_ap, shape_free, waits=()):
        slot, w, cell = ring_slot(waits)
        t = S.add("sync", lambda e, s=slot, a=img_ap: e.dma_start(
            out=ring[:, s, :], in_=a), waits=w, inc=SW, dma=True)
        n = shape_free[0] * shape_free[1]
        view = ring[:, slot, 0:n].rearrange("p (a b) -> p a b", b=shape_free[1])
        return cell, view, t

    def ring_set_free(cell, sem, val):
        if cell[0] is None:
            cell[0] = []
        cell[0].append((sem, val))

    state = {
        "pA_free": [None, None, None], "pB_free": [None, None],
        "pT_free": [None, None],
        "out_t": 0,
        "prev_zs_reader": 0,
        "u_war_pe": 0,
        "h_pe": 0,
        "pe_f1_done": 0,
        "pe_c_mms": 0,
        "d_hT": (0, 0),
        "g_x": [0] * (NCHUNK + 1),
    }

    g_m0 = S.add("gpsimd", lambda e: e.memset(idb[:], 1.0), inc=GP)
    S.add("gpsimd", lambda e: e.affine_select(
        out=idb[:], in_=idb[:], compare_op=OP.is_equal, fill=0.0,
        base=0, pattern=[[-1, 128]], channel_multiplier=1),
        waits=[(GP, g_m0)])
    g_m1 = S.add("gpsimd", lambda e: e.memset(idf[:], 1.0), inc=GP)
    g_idf = S.add("gpsimd", lambda e: e.affine_select(
        out=idf[:], in_=idf[:], compare_op=OP.is_equal, fill=0.0,
        base=0, pattern=[[-1, 128]], channel_multiplier=1),
        waits=[(GP, g_m1)], inc=GP)
    S.add("gpsimd", lambda e: e.memset(ones_row[:], 1.0))
    S.add("gpsimd", lambda e: e.memset(ones_f[:], 1.0))
    g_id = S.add("gpsimd", lambda e: e.memset(eps_t[:], EPS), inc=GP)

    state["g_x"][0] = S.add("sync", lambda e: e.dma_start(
        out=x_sb[:], in_=x_in[0, 0:TC, :].rearrange("(s p) d -> p s d", p=128)),
        inc=SW, dma=True)

    def vec_to_cols(src_row_ap, n, dst_cols, pcol0, waits=(), ldq="sync"):
        nch = n // 128
        g = S.add(ldq, lambda e, a=src_row_ap, n=n: e.dma_start(
            out=row_f[:, 0:n], in_=a), waits=list(waits), inc=SW, dma=True)
        t = None
        for k in range(nch):
            t = S.add("tensor", lambda e, k=k: e.matmul(
                pM[:, pcol0 + k:pcol0 + k + 1], row_f[0:1, k * 128:(k + 1) * 128],
                ones_f[:], start=True, stop=True),
                waits=[(SW, g), (GP, g_id)] if k == 0 else (), inc=PE if k == nch - 1 else None)
        d = S.add("vector", lambda e, nch=nch: e.tensor_copy(
            out=dst_cols, in_=pM[:, pcol0:pcol0 + nch]),
            waits=[(PE, t)], inc=DV)
        return d

    d_wn = vec_to_cols(w_norm[None, :], D, wn_col[:], 0)

    g_v0 = S.add("sync", lambda e: e.dma_start(out=vacc[:], in_=b2[0][None, :]),
                 waits=[(DV, d_wn)], inc=SW, dma=True)
    g_v1 = S.add("sync", lambda e: e.dma_start(out=vtmp[:], in_=b2[1][None, :]),
                 inc=SW, dma=True)
    d_v1 = S.add("vector", lambda e: e.tensor_add(out=vacc[:], in0=vacc[:], in1=vtmp[:]),
                 waits=[(SW, g_v1)], inc=DV)
    g_v2 = S.add("sync", lambda e: e.dma_start(out=vtmp[:], in_=sb2[None, :]),
                 waits=[(DV, d_v1)], inc=SW, dma=True)
    S.add("vector", lambda e: e.tensor_add(out=vacc[:], in0=vacc[:], in1=vtmp[:]),
          waits=[(SW, g_v2)])
    d_vrow = S.add("vector", lambda e: e.tensor_copy(out=vrow[:], in_=vacc[:]), inc=DV)

    pool_a1 = [{"ap": x1_sb[:, 0:2, :], "free": None},
               {"ap": x1_sb[:, 2:4, :], "free": None},
               {"ap": U2[:, 0:2048].rearrange("p (a d) -> p a d", d=1024),
                "free": None},
               {"ap": U2[:, 2048:4096].rearrange("p (a d) -> p a d", d=1024),
                "free": None}]
    pool_a23 = [pool_a1[0], pool_a1[1],
                {"ap": tT_sb.rearrange("p a t -> p (a t)").bitcast(F32)
                 .rearrange("p (a d) -> p a d", d=1024), "free": None}]
    stage_rr = {"a1": 0, "a23": 0}
    bld = {"par": 0}

    def build_slab_c0(pool_name, pool, src_units, img_b, group_src, group_out,
                      store_ap):
        slot, wslot, cell = ring_slot()
        img_v = ring[:, slot, :].rearrange("p (a b) -> p a b", b=img_b)
        t_ld = [0, 0]
        units = [None, None]
        for ui in range(2):
            unit = pool[stage_rr[pool_name] % len(pool)]
            stage_rr[pool_name] += 1
            w = [unit["free"]] if unit["free"] is not None else []
            t_ld[ui] = S.add("sync", lambda e, o=unit["ap"], a=src_units[ui]:
                             e.dma_start(out=o, in_=a), waits=w, inc=SW, dma=True)
            units[ui] = unit
        ev_dv, ev_ac = 0, 0
        last_tp = [0, 0]
        for g in range(8):
            ui = g // 4
            par = bld["par"]
            bld["par"] ^= 1
            wt = [(SW, t_ld[ui]), (GP, g_idf)]
            if state["pB_free"][par] is not None:
                wt.append(state["pB_free"][par])
            srcs = group_src(g, units[ui]["ap"])
            tpe = None
            for k in range(4):
                tpe = S.add("tensor", lambda e, k=k, p=par, s=srcs[k]: e.transpose(
                    pB[p][:, k * 128:(k + 1) * 128], s, idf[:]),
                    waits=wt if k == 0 else (), inc=PE if k == 3 else None)
            last_tp[ui] = tpe
            wev = [(PE, tpe)] + (wslot if g <= 1 else [])
            outv = group_out(g, img_v)
            if par == 0:
                ev_dv = S.add("vector", lambda e, o=outv, p=par: e.tensor_copy(
                    out=o, in_=pB[p][:, 0:512].rearrange("p (k e) -> p k e", e=128)),
                    waits=wev, inc=DV)
                state["pB_free"][par] = (DV, ev_dv)
            else:
                ev_ac = S.add("scalar", lambda e, o=outv, p=par: e.copy(
                    out=o, in_=pB[p][:, 0:512].rearrange("p (k e) -> p k e", e=128)),
                    waits=wev, inc=AC)
                state["pB_free"][par] = (AC, ev_ac)
        for ui in range(2):
            units[ui]["free"] = (PE, last_tp[ui])
        mw = [(DV, ev_dv), (AC, ev_ac)]
        st = S.add("scalar", lambda e, s=slot, a=store_ap: e.dma_start(
            out=a.rearrange("p (a b) -> p a b", b=img_b),
            in_=ring[:, s, :].rearrange("p (a b) -> p a b", b=img_b)),
            waits=list(mw), inc=SW, dma=True)
        ring_set_free(cell, SW, st)
        return cell, img_v, mw, st

    def g_src_q(g, uv):
        return [uv[:, (g % 4) // 2,
                   ((g % 2) * 4 + k) * 128:((g % 2) * 4 + k + 1) * 128]
                for k in range(4)]

    def g_out_q(g, iv):
        col = ((g // 4) * 2 + (g % 4) // 2) * 128
        return iv[:, (g % 2) * 4:(g % 2) * 4 + 4, col:col + 128]

    def g_src_h(g, uv):
        fl = uv.rearrange("p a d -> p (a d)")
        return [fl[:, ((g % 4) * 4 + k) * 128:((g % 4) * 4 + k + 1) * 128]
                for k in range(4)]

    def g_out_h(g, iv):
        return iv[:, (g % 4) * 4:(g % 4) * 4 + 4,
                  (g // 4) * 128:(g // 4) * 128 + 128]

    def img_plain(img_slabs, nslab, shape, src_slice):
        sgl = []
        for k in range(nslab):
            sgl.append(S.add("gpsimd", lambda e, k=k, a=src_slice(k), b=shape[1]: e.dma_start(
                out=img_slabs[k].rearrange("p (a b) -> p a b", b=b), in_=a),
                inc=SG, dma=True))
        return sgl

    p_f1 = []
    p_f2 = []
    for p, (w1s, w2s) in enumerate([(sw1, sw2), (w1[0], w2[0]), (w1[1], w2[1])]):
        p_f1.append(img_plain(
            [t_f1[p][hs] for hs in range(8)], 8, (8, 512),
            lambda hs, s=w1s: s[:, hs * 512:(hs + 1) * 512]
            .rearrange("(dc q) h -> q dc h", q=128)))
        p_f2.append(img_plain(
            [t_f2[p][i] for i in range(8)], 8, (8, 512),
            lambda i, s=w2s: s[(i % 4) * 1024:(i % 4 + 1) * 1024,
                               (i // 4) * 512:(i // 4 + 1) * 512]
            .rearrange("(hc q) d -> q hc d", q=128)))

    def ln_first(s, tile, waits):
        S.add("vector", lambda e, s=s, t=tile: e.reduce_sum(
            out=red[:, s:s + 1], in_=t, axis=AX),
            waits=list(waits), drain=True)
        return S.add("scalar", lambda e, s=s, t=tile: e.activation(
            out=sq_scr[:], in_=t, func=AF.Square, accum_out=sqs[:, s:s + 1]),
            waits=list(waits), inc=AC)

    def ln_rest(n, rms, a_sq):
        d_var, a_sd1, d_t3, a_sd2 = [0] * n, [0] * n, [0] * n, [0] * n
        done = [0] * n
        for s in range(n):
            S.add("vector", lambda e, s=s: e.tensor_scalar_mul(
                out=mu[:, s:s + 1], in0=red[:, s:s + 1], scalar1=1.0 / D))
        for s in range(n):
            S.add("vector", lambda e, s=s: e.tensor_mul(
                out=mu2[:, s:s + 1], in0=mu[:, s:s + 1], in1=mu[:, s:s + 1]),
                waits=[(AC, a_sq[s])])
        for s in range(n):
            d_var[s] = S.add("vector", lambda e, s=s: e.scalar_tensor_tensor(
                out=var[:, s:s + 1], in0=sqs[:, s:s + 1], scalar=1.0 / D,
                in1=mu2[:, s:s + 1], op0=OP.mult, op1=OP.subtract),
                inc=DV, drain=(s == n - 1))
        for s in range(n):
            a_sd1[s] = S.add("scalar", lambda e, s=s: e.activation(
                out=sd1[:, s:s + 1], in_=var[:, s:s + 1], func=AF.Sqrt,
                bias=eps_t[:, 0:1]), waits=[(DV, d_var[s]), (GP, g_id)], inc=AC)
        d_rc = [0] * n
        for s in range(n):
            d_rc[s] = S.add("vector", lambda e, s=s: e.reciprocal(
                out=r1t[:, s:s + 1], in_=sd1[:, s:s + 1]),
                waits=[(AC, a_sd1[s])], inc=DV, drain=(s == n - 1))
        if not rms:
            return d_rc
        for s in range(n):
            S.add("vector", lambda e, s=s: e.tensor_mul(
                out=t2t[:, s:s + 1], in0=var[:, s:s + 1], in1=r1t[:, s:s + 1]))
        for s in range(n):
            d_t3[s] = S.add("vector", lambda e, s=s: e.tensor_mul(
                out=t3t[:, s:s + 1], in0=t2t[:, s:s + 1], in1=r1t[:, s:s + 1]),
                inc=DV, drain=(s == n - 1))
        for s in range(n):
            a_sd2[s] = S.add("scalar", lambda e, s=s: e.activation(
                out=sd2[:, s:s + 1], in_=t3t[:, s:s + 1], func=AF.Sqrt,
                bias=eps_t[:, 0:1]), waits=[(DV, d_t3[s])], inc=AC)
        for s in range(n):
            S.add("vector", lambda e, s=s: e.reciprocal(
                out=r2t[:, s:s + 1], in_=sd2[:, s:s + 1]),
                waits=[(AC, a_sd2[s])], drain=(s == n - 1))
        for s in range(n):
            S.add("vector", lambda e, s=s: e.tensor_mul(
                out=sig[:, s:s + 1], in0=r1t[:, s:s + 1], in1=r2t[:, s:s + 1]))
        for s in range(n):
            S.add("vector", lambda e, s=s: e.tensor_scalar_add(
                out=s1t[:, s:s + 1], in0=r1t[:, s:s + 1], scalar1=1.0))
        for s in range(n):
            done[s] = S.add("vector", lambda e, s=s: e.tensor_mul(
                out=s2t[:, s:s + 1], in0=mu[:, s:s + 1], in1=r1t[:, s:s + 1]),
                inc=DV, drain=(s == n - 1))
        return done

    def ln2_sub_chain(s, a_sq_s):
        S.add("vector", lambda e, s=s: e.tensor_scalar_mul(
            out=mu[:, s:s + 1], in0=red[:, s:s + 1], scalar1=1.0 / D),
            drain=True)
        S.add("vector", lambda e, s=s: e.tensor_mul(
            out=mu2[:, s:s + 1], in0=mu[:, s:s + 1], in1=mu[:, s:s + 1]),
            waits=[(AC, a_sq_s)], drain=True)
        d_var = S.add("vector", lambda e, s=s: e.scalar_tensor_tensor(
            out=var[:, s:s + 1], in0=sqs[:, s:s + 1], scalar=1.0 / D,
            in1=mu2[:, s:s + 1], op0=OP.mult, op1=OP.subtract),
            inc=DV, drain=True)
        a_sd1 = S.add("scalar", lambda e, s=s: e.activation(
            out=sd1[:, s:s + 1], in_=var[:, s:s + 1], func=AF.Sqrt,
            bias=eps_t[:, 0:1]), waits=[(DV, d_var), (GP, g_id)], inc=AC)
        return S.add("vector", lambda e, s=s: e.reciprocal(
            out=r1t[:, s:s + 1], in_=sd1[:, s:s + 1]),
            waits=[(AC, a_sd1)], inc=DV, drain=True)

    def transposes_to_tT(src, sub, wait_dve, scale_cols, wait_ac=None):
        par = sub % 2
        w = [(GP, g_id), (PE, state["h_pe"])]
        if wait_dve is not None:
            w.append((DV, wait_dve))
        if wait_ac is not None:
            w.append((AC, wait_ac))
        if state["pT_free"][par] is not None:
            w += state["pT_free"][par]
        t = None
        for dc in range(8):
            t = S.add("tensor", lambda e, d=dc, p=par, s=src: e.transpose(
                pT[p][:, d * 128:(d + 1) * 128], s[:, d * 128:(d + 1) * 128], idb[:]),
                waits=w if dc == 0 else (), inc=PE if dc == 7 else None)
        state["h_pe"] = t
        eng, SEM = ("vector", DV) if sub % 2 == 0 else ("scalar", AC)
        if scale_cols is None:
            if eng == "vector":
                d_ev = S.add(eng, lambda e, p=par: e.tensor_copy(
                    out=tT_sb[:, :, sub * 128:(sub + 1) * 128],
                    in_=pT[p].rearrange("p (a b) -> p a b", b=128)),
                    waits=[(PE, t)], inc=SEM)
            else:
                d_ev = S.add(eng, lambda e, p=par: e.copy(
                    out=tT_sb[:, :, sub * 128:(sub + 1) * 128],
                    in_=pT[p].rearrange("p (a b) -> p a b", b=128)),
                    waits=[(PE, t)], inc=SEM)
        else:
            for dc in range(8):
                if eng == "vector":
                    d_ev = S.add(eng, lambda e, d=dc, p=par: e.tensor_scalar_mul(
                        out=tT_sb[:, d, sub * 128:(sub + 1) * 128],
                        in0=pT[p][:, d * 128:(d + 1) * 128],
                        scalar1=scale_cols[:, d:d + 1]),
                        waits=[(PE, t)] if dc == 0 else (), inc=SEM if dc == 7 else None)
                else:
                    d_ev = S.add(eng, lambda e, d=dc, p=par: e.activation(
                        out=tT_sb[:, d, sub * 128:(sub + 1) * 128],
                        in_=pT[p][:, d * 128:(d + 1) * 128],
                        func=AF.Copy, scale=scale_cols[:, d:d + 1]),
                        waits=[(PE, t)] if dc == 0 else (), inc=SEM if dc == 7 else None)
        evs = [(SEM, d_ev)]
        state["pT_free"][par] = evs
        return evs

    def emit_ln1(c):
        fw = [(SW, state["g_x"][c])]
        a_sq = [ln_first(s, x_sb[:, s, :], fw) for s in range(NSUB)]
        done = ln_rest(NSUB, True, a_sq)
        d_hs = []
        for s in range(NSUB):
            wh = [(DV, done[s])]
            d_h = S.add("vector", lambda e, s=s: e.tensor_scalar(
                out=hslot(s)[:], in0=x_sb[:, s, :], scalar1=mu[:, s:s + 1],
                scalar2=sig[:, s:s + 1], op0=OP.subtract, op1=OP.mult),
                waits=wh, inc=DV)
            S.add("vector", lambda e, s=s: e.tensor_scalar(
                out=x_sb[:, s, :], in0=x_sb[:, s, :], scalar1=s1t[:, s:s + 1],
                scalar2=s2t[:, s:s + 1], op0=OP.mult, op1=OP.subtract))
            d_hs.append(d_h)
        return d_hs

    def emit_ln1_transposes(d_hs):
        dvm, acm = 0, 0
        for sub in range(NSUB):
            for sem, val in transposes_to_tT(hslot(sub), sub, d_hs[sub],
                                             scale_cols=wn_col):
                if sem is DV:
                    dvm = max(dvm, val)
                else:
                    acm = max(acm, val)
        return [(DV, dvm), (AC, acm)]

    d_hs0 = emit_ln1(0)
    state["d_hT"] = emit_ln1_transposes(d_hs0)

    d_bdt = vec_to_cols(b_dt[None, :], INNER, bdt_col[:], 16, waits=[(DV, d_vrow)])
    d_b1 = 0
    for p, src in enumerate([sb1, b1[0], b1[1]]):
        d_b1 = vec_to_cols(src[None, :], HID, b1c[:, :, p], 48 + 32 * p,
                           waits=[(DV, d_bdt if p == 0 else d_b1)])
    d_g0 = vec_to_cols(gate_w[0][None, :], D, gd0[:], 144, waits=[(DV, d_b1)])
    d_g1 = vec_to_cols(gate_w[1][None, :], D, gd0[:], 152, waits=[(DV, d_g0)])
    d_gd = S.add("vector", lambda e: e.tensor_tensor(
        out=gdiff[:], in0=gd0[:], in1=pM[:, 144:152], op=OP.subtract),
        waits=[(DV, d_g1)], inc=DV)
    state["u_war_pe"] = PE.n

    for c in range(NCHUNK):
        b = c // KCH
        kc = c % KCH
        k0 = kc * TC

        d_hT = state["d_hT"]

        act_uz_wait = [(PE, max(state["u_war_pe"], state["prev_zs_reader"])),
                       (DV, d_gd)]
        first_uz = True
        for es in range(8):
            if c == 0:
                srcs = [W_in[es * 512 + ui * 256:es * 512 + (ui + 1) * 256, :]
                        .rearrange("(a q) d -> q a d", q=128) for ui in range(2)]
                slot_c, wview, mw, st = build_slab_c0(
                    "a1", pool_a1, srcs, 512, g_src_q, g_out_q, t_a1[es])
                state["st_a1"] = st
                wbase = mw
            else:
                wld = [(SW, state["st_a1"])] if c == 1 else []
                slot_c, wview, t_w = ring_load(t_a1[es], (8, 512), waits=wld)
                wbase = [(SW, t_w)]
            last_pe = None
            for el in range(4):
                ec = es * 4 + el
                par = ec % 3
                w = wbase + d_hT
                if state["pA_free"][par] is not None:
                    w.append(state["pA_free"][par])
                for dc in range(8):
                    t_mm = S.add("tensor", lambda e, d=dc, p=par, v=wview, el=el: e.matmul(
                        pA[p][:], v[:, d, el * 128:(el + 1) * 128],
                        tT_sb[:, d, :], start=(d == 0), stop=(d == 7)),
                        waits=w if dc == 0 else (), inc=PE if dc == 7 else None)
                last_pe = t_mm
                tgt = (u_sb, ec) if ec < 16 else (zs_sb, ec - 16)
                d_ev = S.add("scalar", lambda e, p=par, t=tgt: e.activation(
                    out=t[0][:, t[1], :], in_=pA[p][:], func=AF.Silu),
                    waits=[(PE, t_mm)] + (act_uz_wait if first_uz else []), inc=AC)
                first_uz = False
                state["pA_free"][par] = (AC, d_ev)
            ring_set_free(slot_c, PE, last_pe)
        a_uz_all = AC.n

        if c == 0:
            pool_a23[2]["free"] = (PE, PE.n)

        a_lam = []
        bp_emitted = [0]
        scan_emitted = [0]
        d_bp = [0] * 16
        d_m_l = [0] * 16

        def emit_bp(upto, a_lam=a_lam, bp_emitted=bp_emitted, d_bp=d_bp):
            while bp_emitted[0] < min(upto, 16):
                ic = bp_emitted[0]
                d_bp[ic] = S.add("vector", lambda e, i=ic: e.scalar_tensor_tensor(
                    out=b_scr2[i % 2][:],
                    in0=lam_v[:, i, :], scalar=1.0, in1=u_sb[:, i, :],
                    op0=OP.subtract, op1=OP.mult),
                    waits=[(AC, a_lam[ic])], inc=DV)
                bp_emitted[0] += 1

        def emit_scan_ops(upto, kc=kc, scan_emitted=scan_emitted, d_bp=d_bp,
                          d_m_l=d_m_l):
            emit_bp(upto + 1)
            while scan_emitted[0] < upto:
                ic = scan_emitted[0]
                init = 0.0 if kc == 0 else carry[:, ic:ic + 1]
                S.add("vector", lambda e, i=ic, init=init: e.tensor_tensor_scan(
                    out=s_v[:, i, :], data0=lam_v[:, i, :],
                    data1=b_scr2[i % 2][:],
                    initial=init, op0=OP.mult, op1=OP.add),
                    waits=[(DV, d_bp[ic])], drain=True)
                d_m_l[ic] = S.add("vector", lambda e, i=ic: e.tensor_mul(
                    out=zs_sb[:, i, :], in0=s_v[:, i, :], in1=zs_sb[:, i, :]),
                    inc=DV)
                scan_emitted[0] += 1

        for js in range(8):
            if c == 0:
                srcs = [W_dt[js * 256 + jl * 128:js * 256 + (jl + 1) * 128, :]
                        .rearrange("q (a d) -> q a d", d=1024) for jl in range(2)]
                slot_c, wview, mw, st = build_slab_c0(
                    "a23", pool_a23, srcs, 256, g_src_h, g_out_h, t_a2[js])
                state["st_a2"] = st
                wbase = mw
            else:
                wld = [(SW, state["st_a2"])] if c == 1 else []
                slot_c, wview, t_w = ring_load(t_a2[js], (16, 256), waits=wld)
                wbase = [(SW, t_w)]
            last_pe = None
            for jl in range(2):
                j = js * 2 + jl
                par = j % 3
                w = wbase + [(AC, a_uz_all)]
                if state["pA_free"][par] is not None:
                    w.append(state["pA_free"][par])
                for ic in range(16):
                    t_mm = S.add("tensor", lambda e, i=ic, p=par, v=wview, jl=jl: e.matmul(
                        pA[p][:], v[:, i, jl * 128:(jl + 1) * 128],
                        u_sb[:, i, :], start=(i == 0), stop=(i == 15)),
                        waits=w if ic == 0 else (), inc=PE if ic == 15 else None)
                last_pe = t_mm
                wlam = [(PE, t_mm)]
                if c > 0:
                    wlam.append((SO, state["out_t"]))
                a_ev = S.add("scalar", lambda e, p=par, j=j: e.activation(
                    out=lam_v[:, j, :], in_=pA[p][:], func=AF.Sigmoid,
                    bias=bdt_col[:, j:j + 1]), waits=wlam, inc=AC)
                a_lam.append(a_ev)
                state["pA_free"][par] = (AC, a_ev)
                emit_scan_ops(len(a_lam) - 1)
            ring_set_free(slot_c, PE, last_pe)
        pe_a2_all = PE.n
        emit_scan_ops(16)
        d_m_all = DV.n
        for ic in range(16):
            S.add("vector", lambda e, i=ic: e.tensor_copy(
                out=carry[:, i:i + 1], in_=s_v[:, i, TC - 1:TC]))

        S.add("scalar", lambda e: e.activation(
            out=sd1[:, 0:1], in_=eps_t[:, 0:1], func=AF.Sqrt),
            waits=[(GP, g_id)])
        if c == 0:
            slabs = []
            dvm, acm = 0, 0
            for ks in range(4):
                dh, ih = ks // 2, ks % 2
                srcs = [W_out[dh * 512 + up * 256:dh * 512 + (up + 1) * 256,
                              ih * 1024:(ih + 1) * 1024]
                        .rearrange("(a q) i -> q a i", q=128) for up in range(2)]
                cell, iv, mw, st = build_slab_c0(
                    "a23", pool_a23, srcs, 512, g_src_q, g_out_q, t_c[ks])
                state["st_c"] = st
                dvm, acm = max(dvm, mw[0][1]), max(acm, mw[1][1])
                slabs.append((cell, iv, None))
            wC = [(DV, dvm), (AC, acm)]
        else:
            wld = [(SW, state["st_c"])] if c == 1 else []
            slabs = [ring_load(t_c[k], (8, 512), waits=wld) for k in range(4)]
            wC = [(SW, slabs[3][2])]
        first_cg = True
        d_g2 = [0] * NSUB
        for sub in range(NSUB):
            for dh in range(2):
                par = (2 * sub + dh) % 2
                w = wC + ([] if first_cg else [(DV, d_m_all)])
                if state["pB_free"][par] is not None:
                    w.append(state["pB_free"][par])
                t_mm = None
                for ih in range(2):
                    v = slabs[2 * dh + ih][1]
                    for il in range(8):
                        ic = ih * 8 + il
                        mmw = list(w) if ic == 0 else []
                        if first_cg:
                            mmw.append((DV, d_m_l[ic]))
                        t_mm = S.add("tensor",
                                     lambda e, i=ic, il=il, v=v, p=par, sub=sub,
                                     first=(ic == 0), last=(ic == 15):
                                     e.matmul(pB[p][:],
                                              zs_sb[:, i, sub * 128:(sub + 1) * 128],
                                              v[:, il, :], start=first, stop=last),
                                     waits=mmw, inc=PE if ic == 15 else None)
                first_cg = False
                state["pe_c_mms"] = t_mm
                d_x1 = S.add("vector", lambda e, p=par, sub=sub, dh=dh: e.tensor_tensor(
                    out=x1_sb[:, sub, dh * 512:(dh + 1) * 512],
                    in0=x_sb[:, sub, dh * 512:(dh + 1) * 512],
                    in1=pB[p][:], op=OP.subtract),
                    waits=[(PE, t_mm)], inc=DV)
                state["pB_free"][par] = (DV, d_x1)
                if dh == 1:
                    state[f"d_x1_{sub}"] = d_x1
                    state["d_x1_last"] = d_x1
                    a_sq_s = ln_first(sub, x1_sb[:, sub, :], [(DV, d_x1)])
                    if sub < NSUB - 1:
                        d_g2[sub] = ln2_sub_chain(sub, a_sq_s)
                    else:
                        sq_last = a_sq_s
        for cell, _, _ in slabs:
            ring_set_free(cell, PE, state["pe_c_mms"])
        state["prev_zs_reader"] = state["pe_c_mms"]

        d_gv = [0] * NSUB
        for sub in range(NSUB - 1):
            d_gv[sub] = S.add("vector", lambda e, s=sub: e.tensor_scalar(
                out=h4[:, s, :], in0=x1_sb[:, s, :], scalar1=mu[:, s:s + 1],
                scalar2=r1t[:, s:s + 1], op0=OP.subtract, op1=OP.mult),
                waits=[(DV, d_g2[sub]), (PE, state["h_pe"])], inc=DV)
        ls = NSUB - 1
        d_g2[ls] = ln2_sub_chain(ls, sq_last)
        d_gv[ls] = S.add("vector", lambda e, s=ls: e.tensor_scalar(
            out=h4[:, s, :], in0=x1_sb[:, s, :], scalar1=mu[:, s:s + 1],
            scalar2=r1t[:, s:s + 1], op0=OP.subtract, op1=OP.mult),
            waits=[(DV, d_g2[ls]), (PE, state["h_pe"])], inc=DV)
        dvm, acm = 0, 0
        for sub in range(NSUB):
            for sem, val in transposes_to_tT(h4[:, sub, :], sub, d_gv[sub],
                                             scale_cols=None):
                if sem is DV:
                    dvm = max(dvm, val)
                else:
                    acm = max(acm, val)
        d_gT = [(DV, dvm), (AC, acm)]

        d_hs_next = None
        if c + 1 < NCHUNK:
            b2_, kc2 = (c + 1) // KCH, (c + 1) % KCH
            state["g_x"][c + 1] = S.add("sync", lambda e, b=b2_, k0=kc2 * TC: e.dma_start(
                out=x_sb[:], in_=x_in[b, k0:k0 + TC, :].rearrange("(s p) d -> p s d", p=128)),
                waits=[(DV, state["d_x1_last"])], inc=SW, dma=True)
            d_hs_next = emit_ln1(c + 1)

        t_sd = None
        for dc in range(8):
            t_sd = S.add("tensor", lambda e, d=dc: e.matmul(
                pM[0:1, :], gdiff[:, d:d + 1], tT_sb[:, d, :],
                start=(d == 0), stop=(d == 7)),
                waits=d_gT + [(DV, d_gd)] if dc == 0 else (),
                inc=PE if dc == 7 else None)

        def emit_mask_section():
            S.add("vector", lambda e: e.tensor_single_scalar(
                out=se0[:], in_=pM[0:1, :], scalar=0.0, op=OP.is_gt),
                waits=[(PE, t_sd)])
            d_mrow = S.add("vector", lambda e: e.tensor_copy(out=mrow[:], in_=se0[:]), inc=DV)
            t_mrep = S.add("tensor", lambda e: e.matmul(
                pM[:, :], ones_row[:], mrow[:], start=True, stop=True),
                waits=[(DV, d_mrow)], inc=PE)
            d_mrep = S.add("vector", lambda e: e.tensor_copy(out=mrep_sb[:], in_=pM[:]),
                           waits=[(PE, t_mrep)], inc=DV)
            d_mrep0 = S.add("vector", lambda e: e.tensor_scalar(
                out=mrep0_sb[:], in0=mrep_sb[:], scalar1=-1.0, scalar2=1.0,
                op0=OP.mult, op1=OP.add), inc=DV)
            return d_mrep, d_mrep0

        pe_f2_prev = 0
        for p in range(3):
            if p == 0:
                rhs_view, d_rhs = tT_sb, list(d_gT)
                d_rhs_dc = None
            else:
                msk = mrep0_sb if p == 1 else mrep_sb
                d_msk = d_mrep0 if p == 1 else d_mrep
                d_rhs_dc = []
                for dc in range(8):
                    wm = [(DV, d_msk)]
                    if p == 2:
                        wm.append((PE, state["pe_f1_done"]))
                    d_b = S.add("vector", lambda e, d=dc, m=msk: e.tensor_mul(
                        out=gTe_v[:, d, :], in0=tT_sb[:, d, :], in1=m[:]),
                        waits=wm if dc == 0 else (), inc=DV)
                    d_rhs_dc.append(d_b)
                d_rhs = [(DV, d_rhs_dc[7])]
                rhs_view = gTe_v
            sl2_next = []
            first_f1 = True
            for hs in range(8):
                wslab = [(SG, p_f1[p][hs])] if c == 0 else []
                slot_c, wview, t_w = ring_load(t_f1[p][hs], (8, 512), waits=wslab)
                if hs >= 4:
                    sl2_next.append(ring_load(
                        t_f2[p][hs - 4], (8, 512),
                        waits=[(SG, p_f2[p][hs - 4])] if c == 0 else ()))
                last_pe = None
                for hl in range(4):
                    hc = hs * 4 + hl
                    par = hc % 3
                    per_dc = d_rhs_dc is not None and hs == 0 and hl == 0
                    w = [(SW, t_w)] + ([] if per_dc else d_rhs)
                    if first_f1:
                        w.append((PE, pe_f2_prev))
                        first_f1 = False
                    if state["pA_free"][par] is not None:
                        w.append(state["pA_free"][par])
                    for dc in range(8):
                        mmw = list(w) if dc == 0 else []
                        if per_dc:
                            mmw.append((DV, d_rhs_dc[dc]))
                        t_mm = S.add("tensor", lambda e, d=dc, p2=par, v=wview, hl=hl, rv=rhs_view:
                                     e.matmul(pA[p2][:], v[:, d, hl * 128:(hl + 1) * 128],
                                              rv[:, d, :], start=(d == 0), stop=(d == 7)),
                                     waits=mmw, inc=PE if dc == 7 else None)
                    last_pe = t_mm
                    d_ev = S.add("scalar", lambda e, p2=par, hc=hc, p=p: e.activation(
                        out=hrelu_v[:, hc, :], in_=pA[p2][:], func=AF.Relu,
                        bias=b1c[:, hc:hc + 1, p]), waits=[(PE, t_mm)], inc=AC)
                    state["pA_free"][par] = (AC, d_ev)
                ring_set_free(slot_c, PE, last_pe)
            a_f1_all = AC.n
            state["pe_f1_done"] = PE.n
            if p == 0:
                d_mrep, d_mrep0 = emit_mask_section()

            if p == 2 and d_hs_next is not None:
                state["d_hT"] = emit_ln1_transposes(d_hs_next)
                state["u_war_pe"] = PE.n

            banks = [pA[0], pA[1], pA[2], pB[0]]
            for dq in range(2):
                sl2 = sl2_next
                sl2_next = []
                first_pending = [None] * 4
                for sub in range(4):
                    w = [(AC, a_f1_all)]
                    fr = state["pA_free"][sub] if sub < 3 else state["pB_free"][0]
                    if fr is not None:
                        w.append(fr)
                    if p == 0:
                        w.append((DV, d_vrow))
                        S.add("tensor", lambda e, b=banks[sub], dq=dq: e.matmul(
                            b[:], ones_row[:], vrow[0:1, dq * 512:(dq + 1) * 512],
                            start=True, stop=False), waits=w)
                    else:
                        first_pending[sub] = w
                t_stop = [0] * 4
                for ks in range(4):
                    cell, v, t_w = sl2[ks]
                    if dq == 0:
                        sl2_next.append(ring_load(
                            t_f2[p][4 + ks], (8, 512),
                            waits=[(SG, p_f2[p][4 + ks])] if c == 0 else ()))
                    for hl in range(8):
                        hc = ks * 8 + hl
                        for sub in range(4):
                            mmw = [(SW, t_w)] if (hl == 0 and sub == 0) else []
                            st = False
                            if first_pending[sub] is not None:
                                mmw = mmw + first_pending[sub]
                                first_pending[sub] = None
                                st = True
                            last = hc == 31
                            t_mm = S.add("tensor",
                                         lambda e, b=banks[sub], hc=hc, hl=hl,
                                         v=v, sub=sub, st=st, last=last:
                                         e.matmul(b[:],
                                                  hrelu_v[:, hc, sub * 128:(sub + 1) * 128],
                                                  v[:, hl, :], start=st, stop=last),
                                         waits=mmw,
                                         inc=PE if (last or (hl == 7 and sub == 3)) else None)
                            if last:
                                t_stop[sub] = t_mm
                    ring_set_free(cell, PE, PE.n)
                for sub in range(4):
                    if p == 0:
                        d_ev = S.add("vector", lambda e, b=banks[sub], sub=sub, dq=dq:
                                     e.tensor_tensor(
                                         out=facc_v[:, sub, dq * 512:(dq + 1) * 512],
                                         in0=x1_sb[:, sub, dq * 512:(dq + 1) * 512],
                                         in1=b[:], op=OP.add),
                                     waits=[(PE, t_stop[sub])], inc=DV)
                    else:
                        d_ev = S.add("vector", lambda e, b=banks[sub], sub=sub, dq=dq:
                                     e.tensor_tensor(
                                         out=facc_v[:, sub, dq * 512:(dq + 1) * 512],
                                         in0=facc_v[:, sub, dq * 512:(dq + 1) * 512],
                                         in1=b[:], op=OP.add),
                                     waits=[(PE, t_stop[sub])], inc=DV)
                    if sub < 3:
                        state["pA_free"][sub] = (DV, d_ev)
                    else:
                        state["pB_free"][0] = (DV, d_ev)
            pe_f2_prev = PE.n
        d_facc = DV.n

        state["out_t"] = S.add("gpsimd", lambda e, b=b, k0=k0: e.dma_start(
            out=y_out[b, k0:k0 + TC, :].rearrange("(s p) d -> p s d", p=128),
            in_=facc_v[:]),
            waits=[(DV, d_facc)], inc=SO, dma=True)

    S.add("gpsimd", lambda e: None, waits=[(SO, SO.n)])

    @block.sync
    def _(eng):
        S.replay("sync", eng)

    @block.tensor
    def _(eng):
        S.replay("tensor", eng)

    @block.vector
    def _(eng):
        S.replay("vector", eng)

    @block.scalar
    def _(eng):
        S.replay("scalar", eng)

    @block.gpsimd
    def _(eng):
        S.replay("gpsimd", eng)

    stack.close()
    return nc


_NC_CACHE = None


def kernel(**inputs):
    global _NC_CACHE
    if _NC_CACHE is None:
        _NC_CACHE = build_program()
    nc = _NC_CACHE

    x = np.ascontiguousarray(inputs["L0_attn_x"])
    shards = np.split(x, NCORES, axis=0)
    base = {k: np.ascontiguousarray(v) for k, v in inputs.items() if k != "L0_attn_x"}
    in_maps = []
    for c in range(NCORES):
        m = dict(base)
        m["L0_attn_x"] = shards[c]
        in_maps.append(m)
    res = run_bass_kernel_spmd(nc, in_maps, core_ids=list(range(NCORES)))
    out = np.concatenate([np.asarray(r["y"]) for r in res.results], axis=0)
    return out.astype(np.float32)


# version 44 (final)
# speedup vs baseline: 1.0551x; 1.0028x over previous
import numpy as np
import concourse.bass as bass
import concourse.mybir as mybir
from concourse.bass_utils import run_bass_kernel_spmd

F32 = mybir.dt.float32
BF16 = mybir.dt.bfloat16
AF = mybir.ActivationFunctionType
OP = mybir.AluOpType
AX = mybir.AxisListType.X

NCORES = 8
BPC = 4
SEQ = 2048
D = 1024
INNER = 2048
HID = 4096
TC = 512
NSUB = TC // 128
KCH = SEQ // TC
NCHUNK = BPC * KCH
EPS = 1e-5
NRING = 7

ENGS = ("sync", "tensor", "vector", "scalar", "gpsimd")


class Sem:
    def __init__(self, handle):
        self.h = handle
        self.n = 0


class Sched:
    def __init__(self):
        self.ops = {e: [] for e in ENGS}

    def add(self, eng, fn, waits=(), inc=None, dma=False, drain=False):
        amt = 16 if dma else 1
        incs = []
        if inc is not None:
            for s in (inc if isinstance(inc, (list, tuple)) else [inc]):
                s.n += amt
                incs.append((s, amt))
        self.ops[eng].append((fn, tuple(waits), tuple(incs), drain))
        return incs[0][0].n if incs else None

    def replay(self, eng_name, eng):
        floor = {}
        for fn, waits, incs, drain in self.ops[eng_name]:
            for s, v in waits:
                if s == "CELL":
                    assert v[0] is not None, "unresolved ring free cell"
                    pairs = v[0]
                else:
                    pairs = [(s, v)]
                for s2, v2 in pairs:
                    if floor.get(id(s2), 0) < v2:
                        eng.wait_ge(s2.h, v2)
                        floor[id(s2)] = v2
            inst = fn(eng)
            for s, amt in incs:
                inst.then_inc(s.h, amt)
            if drain:
                eng.drain()


def build_program():
    nc = bass.Bass("TRN2", target_bir_lowering=False, debug=False)

    x_in = nc.dram_tensor("L0_attn_x", [BPC, SEQ, D], F32, kind="ExternalInput").ap()
    w_norm = nc.dram_tensor("L0_attn_w_norm", [D], F32, kind="ExternalInput").ap()
    W_in = nc.dram_tensor("L0_attn_W_in", [2 * INNER, D], F32, kind="ExternalInput").ap()
    W_dt = nc.dram_tensor("L0_attn_W_dt", [INNER, INNER], F32, kind="ExternalInput").ap()
    b_dt = nc.dram_tensor("L0_attn_b_dt", [INNER], F32, kind="ExternalInput").ap()
    W_out = nc.dram_tensor("L0_attn_W_out", [D, INNER], F32, kind="ExternalInput").ap()
    gate_w = nc.dram_tensor("L0_ffn_gate_w", [2, D], F32, kind="ExternalInput").ap()
    w1 = nc.dram_tensor("L0_ffn_w1", [2, D, HID], F32, kind="ExternalInput").ap()
    b1 = nc.dram_tensor("L0_ffn_b1", [2, HID], F32, kind="ExternalInput").ap()
    w2 = nc.dram_tensor("L0_ffn_w2", [2, HID, D], F32, kind="ExternalInput").ap()
    b2 = nc.dram_tensor("L0_ffn_b2", [2, D], F32, kind="ExternalInput").ap()
    sw1 = nc.dram_tensor("L0_ffn_sw1", [D, HID], F32, kind="ExternalInput").ap()
    sb1 = nc.dram_tensor("L0_ffn_sb1", [HID], F32, kind="ExternalInput").ap()
    sw2 = nc.dram_tensor("L0_ffn_sw2", [HID, D], F32, kind="ExternalInput").ap()
    sb2 = nc.dram_tensor("L0_ffn_sb2", [D], F32, kind="ExternalInput").ap()
    y_out = nc.dram_tensor("y", [BPC, SEQ, D], F32, kind="ExternalOutput").ap()

    t_a1 = nc.dram_tensor("t_a1", [8, 128, 4096], BF16, kind="Internal").ap()
    t_a2 = nc.dram_tensor("t_a2", [8, 128, 4096], BF16, kind="Internal").ap()
    t_c = nc.dram_tensor("t_c", [4, 128, 4096], BF16, kind="Internal").ap()
    t_f1 = nc.dram_tensor("t_f1", [3, 8, 128, 4096], BF16, kind="Internal").ap()
    t_f2 = nc.dram_tensor("t_f2", [3, 8, 128, 4096], BF16, kind="Internal").ap()

    from contextlib import ExitStack
    stack = ExitStack()
    E = stack.enter_context

    x_sb = E(nc.sbuf_tensor("x_sb", [128, NSUB, D], F32))
    x1_sb = E(nc.sbuf_tensor("x1_sb", [128, NSUB, D], F32))
    tT_sb = E(nc.sbuf_tensor("tT_sb", [128, 8, TC], BF16))
    u_sb = E(nc.sbuf_tensor("u_sb", [128, 16, TC], BF16))
    zs_sb = E(nc.sbuf_tensor("zs_sb", [128, 16, TC], BF16))
    U1 = E(nc.sbuf_tensor("U1", [128, 8192], F32))
    U2 = E(nc.sbuf_tensor("U2", [128, 8192], F32))
    ring = E(nc.sbuf_tensor("ring", [128, NRING, 4096], BF16))
    b_scr = E(nc.sbuf_tensor("b_scr", [128, 2, TC], BF16))
    b_scr2 = [b_scr[:, 0, :], b_scr[:, 1, :]]
    sq_scr = E(nc.sbuf_tensor("sq_scr", [128, D], BF16))
    idb = E(nc.sbuf_tensor("idb", [128, 128], BF16))
    idf = E(nc.sbuf_tensor("idf", [128, 128], F32))
    ones_row = E(nc.sbuf_tensor("ones_row", [1, 128], BF16))
    ones_f = E(nc.sbuf_tensor("ones_f", [1, 1], F32))
    eps_t = E(nc.sbuf_tensor("eps_t", [128, 1], F32))
    wn_col = E(nc.sbuf_tensor("wn_col", [128, 8], F32))
    bdt_col = E(nc.sbuf_tensor("bdt_col", [128, 16], F32))
    b1c = E(nc.sbuf_tensor("b1c", [128, 32, 3], F32))
    vrow = E(nc.sbuf_tensor("vrow", [1, D], BF16))
    gd0 = E(nc.sbuf_tensor("gd0", [128, 8], F32))
    gdiff = E(nc.sbuf_tensor("gdiff", [128, 8], BF16))
    carry = E(nc.sbuf_tensor("carry", [128, 16], F32))
    red = E(nc.sbuf_tensor("red", [128, 4], F32))
    sqs = E(nc.sbuf_tensor("sqs", [128, 4], F32))
    mu = E(nc.sbuf_tensor("mu", [128, 4], F32))
    mu2 = E(nc.sbuf_tensor("mu2", [128, 4], F32))
    var = E(nc.sbuf_tensor("var", [128, 4], F32))
    sd1 = E(nc.sbuf_tensor("sd1", [128, 4], F32))
    r1t = E(nc.sbuf_tensor("r1t", [128, 4], F32))
    t2t = E(nc.sbuf_tensor("t2t", [128, 4], F32))
    t3t = E(nc.sbuf_tensor("t3t", [128, 4], F32))
    sd2 = E(nc.sbuf_tensor("sd2", [128, 4], F32))
    r2t = E(nc.sbuf_tensor("r2t", [128, 4], F32))
    sig = E(nc.sbuf_tensor("sig", [128, 4], F32))
    s1t = E(nc.sbuf_tensor("s1t", [128, 4], F32))
    s2t = E(nc.sbuf_tensor("s2t", [128, 4], F32))
    se0 = E(nc.sbuf_tensor("se0", [1, TC], F32))
    mrow = E(nc.sbuf_tensor("mrow", [1, TC], BF16))
    mrep_sb = E(nc.sbuf_tensor("mrep_sb", [128, TC], BF16))
    mrep0_sb = E(nc.sbuf_tensor("mrep0_sb", [128, TC], BF16))

    _u_row = zs_sb[0:1, :, :].rearrange("p a t -> p (a t)")
    vacc = _u_row[:, 0:2048].bitcast(F32)
    vtmp = _u_row[:, 2048:4096].bitcast(F32)
    row_f = _u_row[:, 0:8192].bitcast(F32)
    s_v = U1[:, 0:8192].rearrange("p (i t) -> p i t", t=TC)
    hrelu_v = U1[:, 0:8192].bitcast(BF16).rearrange("p (h t) -> p h t", t=TC)
    lam_v = U2[:, 0:8192].rearrange("p (i t) -> p i t", t=TC)
    facc_v = U2[:, 0:4096].rearrange("p (s d) -> p s d", d=D)
    gTe_v = U2[:, 4096:6144].bitcast(BF16).rearrange("p (c t) -> p c t", t=TC)
    gTe_v2 = U2[:, 6144:8192].bitcast(BF16).rearrange("p (c t) -> p c t", t=TC)

    def hslot(sub):
        return u_sb[:, 2 * sub:2 * sub + 2, :].rearrange("p a t -> p (a t)")

    h4 = u_sb[:, 8:16, :].rearrange("p a t -> p (a t)").rearrange(
        "p (a d) -> p a d", d=D)

    pA = [E(nc.psum_tensor(f"pA{i}", [128, 512], F32)) for i in range(3)]
    pB = [E(nc.psum_tensor(f"pB{i}", [128, 512], F32)) for i in range(2)]
    pT = [E(nc.psum_tensor(f"pT{i}", [128, 1024], BF16)) for i in range(2)]
    pM = E(nc.psum_tensor("pM", [128, 512], F32))

    SW = Sem(E(nc.semaphore("SW")))
    SS = Sem(E(nc.semaphore("SS")))
    SG = Sem(E(nc.semaphore("SG")))
    SO = Sem(E(nc.semaphore("SO")))
    PE = Sem(E(nc.semaphore("PE")))
    DV = Sem(E(nc.semaphore("DV")))
    AC = Sem(E(nc.semaphore("AC")))
    GP = Sem(E(nc.semaphore("GP")))

    block = E(nc.Block())
    S = Sched()

    ring_free = [None] * NRING
    ring_ptr = [0]

    def ring_slot(waits=()):
        slot = ring_ptr[0]
        ring_ptr[0] = (slot + 1) % NRING
        w = list(waits)
        if ring_free[slot] is not None:
            w.append(("CELL", ring_free[slot]))
        cell = [None]
        ring_free[slot] = cell
        return slot, w, cell

    def ring_load(img_ap, shape_free, waits=()):
        slot, w, cell = ring_slot(waits)
        t = S.add("sync", lambda e, s=slot, a=img_ap: e.dma_start(
            out=ring[:, s, :], in_=a), waits=w, inc=SW, dma=True)
        n = shape_free[0] * shape_free[1]
        view = ring[:, slot, 0:n].rearrange("p (a b) -> p a b", b=shape_free[1])
        return cell, view, t

    def ring_set_free(cell, sem, val):
        if cell[0] is None:
            cell[0] = []
        cell[0].append((sem, val))

    state = {
        "pA_free": [None, None, None], "pB_free": [None, None],
        "pT_free": [None, None],
        "out_t": 0,
        "prev_zs_reader": 0,
        "u_war_pe": 0,
        "h_pe": 0,
        "pe_f1_done": 0,
        "pe_c_mms": 0,
        "d_hT": (0, 0),
        "g_x": [0] * (NCHUNK + 1),
    }

    g_m0 = S.add("gpsimd", lambda e: e.memset(idb[:], 1.0), inc=GP)
    S.add("gpsimd", lambda e: e.affine_select(
        out=idb[:], in_=idb[:], compare_op=OP.is_equal, fill=0.0,
        base=0, pattern=[[-1, 128]], channel_multiplier=1),
        waits=[(GP, g_m0)])
    g_m1 = S.add("gpsimd", lambda e: e.memset(idf[:], 1.0), inc=GP)
    g_idf = S.add("gpsimd", lambda e: e.affine_select(
        out=idf[:], in_=idf[:], compare_op=OP.is_equal, fill=0.0,
        base=0, pattern=[[-1, 128]], channel_multiplier=1),
        waits=[(GP, g_m1)], inc=GP)
    S.add("gpsimd", lambda e: e.memset(ones_row[:], 1.0))
    S.add("gpsimd", lambda e: e.memset(ones_f[:], 1.0))
    g_id = S.add("gpsimd", lambda e: e.memset(eps_t[:], EPS), inc=GP)

    state["g_x"][0] = S.add("sync", lambda e: e.dma_start(
        out=x_sb[:], in_=x_in[0, 0:TC, :].rearrange("(s p) d -> p s d", p=128)),
        inc=SW, dma=True)

    def vec_to_cols(src_row_ap, n, dst_cols, pcol0, waits=(), ldq="sync"):
        nch = n // 128
        g = S.add(ldq, lambda e, a=src_row_ap, n=n: e.dma_start(
            out=row_f[:, 0:n], in_=a), waits=list(waits), inc=SW, dma=True)
        t = None
        for k in range(nch):
            t = S.add("tensor", lambda e, k=k: e.matmul(
                pM[:, pcol0 + k:pcol0 + k + 1], row_f[0:1, k * 128:(k + 1) * 128],
                ones_f[:], start=True, stop=True),
                waits=[(SW, g), (GP, g_id)] if k == 0 else (), inc=PE if k == nch - 1 else None)
        d = S.add("vector", lambda e, nch=nch: e.tensor_copy(
            out=dst_cols, in_=pM[:, pcol0:pcol0 + nch]),
            waits=[(PE, t)], inc=DV)
        return d

    d_wn = vec_to_cols(w_norm[None, :], D, wn_col[:], 0)

    g_v0 = S.add("sync", lambda e: e.dma_start(out=vacc[:], in_=b2[0][None, :]),
                 waits=[(DV, d_wn)], inc=SW, dma=True)
    g_v1 = S.add("sync", lambda e: e.dma_start(out=vtmp[:], in_=b2[1][None, :]),
                 inc=SW, dma=True)
    d_v1 = S.add("vector", lambda e: e.tensor_add(out=vacc[:], in0=vacc[:], in1=vtmp[:]),
                 waits=[(SW, g_v1)], inc=DV)
    g_v2 = S.add("sync", lambda e: e.dma_start(out=vtmp[:], in_=sb2[None, :]),
                 waits=[(DV, d_v1)], inc=SW, dma=True)
    S.add("vector", lambda e: e.tensor_add(out=vacc[:], in0=vacc[:], in1=vtmp[:]),
          waits=[(SW, g_v2)])
    d_vrow = S.add("vector", lambda e: e.tensor_copy(out=vrow[:], in_=vacc[:]), inc=DV)

    pool_a1 = [{"ap": x1_sb[:, 0:2, :], "free": None},
               {"ap": x1_sb[:, 2:4, :], "free": None},
               {"ap": U2[:, 0:2048].rearrange("p (a d) -> p a d", d=1024),
                "free": None},
               {"ap": U2[:, 2048:4096].rearrange("p (a d) -> p a d", d=1024),
                "free": None}]
    pool_a23 = [pool_a1[0], pool_a1[1],
                {"ap": tT_sb.rearrange("p a t -> p (a t)").bitcast(F32)
                 .rearrange("p (a d) -> p a d", d=1024), "free": None}]
    stage_rr = {"a1": 0, "a23": 0}
    bld = {"par": 0}

    def build_slab_c0(pool_name, pool, src_units, img_b, group_src, group_out,
                      store_ap):
        slot, wslot, cell = ring_slot()
        img_v = ring[:, slot, :].rearrange("p (a b) -> p a b", b=img_b)
        t_ld = [0, 0]
        units = [None, None]
        for ui in range(2):
            unit = pool[stage_rr[pool_name] % len(pool)]
            stage_rr[pool_name] += 1
            w = [unit["free"]] if unit["free"] is not None else []
            t_ld[ui] = S.add("sync", lambda e, o=unit["ap"], a=src_units[ui]:
                             e.dma_start(out=o, in_=a), waits=w, inc=SW, dma=True)
            units[ui] = unit
        ev_dv, ev_ac = 0, 0
        last_tp = [0, 0]
        for g in range(8):
            ui = g // 4
            par = bld["par"]
            bld["par"] ^= 1
            wt = [(SW, t_ld[ui]), (GP, g_idf)]
            if state["pB_free"][par] is not None:
                wt.append(state["pB_free"][par])
            srcs = group_src(g, units[ui]["ap"])
            tpe = None
            for k in range(4):
                tpe = S.add("tensor", lambda e, k=k, p=par, s=srcs[k]: e.transpose(
                    pB[p][:, k * 128:(k + 1) * 128], s, idf[:]),
                    waits=wt if k == 0 else (), inc=PE if k == 3 else None)
            last_tp[ui] = tpe
            wev = [(PE, tpe)] + (wslot if g <= 1 else [])
            outv = group_out(g, img_v)
            if par == 0:
                ev_dv = S.add("vector", lambda e, o=outv, p=par: e.tensor_copy(
                    out=o, in_=pB[p][:, 0:512].rearrange("p (k e) -> p k e", e=128)),
                    waits=wev, inc=DV)
                state["pB_free"][par] = (DV, ev_dv)
            else:
                ev_ac = S.add("scalar", lambda e, o=outv, p=par: e.copy(
                    out=o, in_=pB[p][:, 0:512].rearrange("p (k e) -> p k e", e=128)),
                    waits=wev, inc=AC)
                state["pB_free"][par] = (AC, ev_ac)
        for ui in range(2):
            units[ui]["free"] = (PE, last_tp[ui])
        mw = [(DV, ev_dv), (AC, ev_ac)]
        st = S.add("scalar", lambda e, s=slot, a=store_ap: e.dma_start(
            out=a.rearrange("p (a b) -> p a b", b=img_b),
            in_=ring[:, s, :].rearrange("p (a b) -> p a b", b=img_b)),
            waits=list(mw), inc=SS, dma=True)
        ring_set_free(cell, SS, st)
        return cell, img_v, mw, st

    def g_src_q(g, uv):
        return [uv[:, (g % 4) // 2,
                   ((g % 2) * 4 + k) * 128:((g % 2) * 4 + k + 1) * 128]
                for k in range(4)]

    def g_out_q(g, iv):
        col = ((g // 4) * 2 + (g % 4) // 2) * 128
        return iv[:, (g % 2) * 4:(g % 2) * 4 + 4, col:col + 128]

    def g_src_h(g, uv):
        fl = uv.rearrange("p a d -> p (a d)")
        return [fl[:, ((g % 4) * 4 + k) * 128:((g % 4) * 4 + k + 1) * 128]
                for k in range(4)]

    def g_out_h(g, iv):
        return iv[:, (g % 4) * 4:(g % 4) * 4 + 4,
                  (g // 4) * 128:(g // 4) * 128 + 128]

    def img_plain(img_slabs, nslab, shape, src_slice):
        sgl = []
        for k in range(nslab):
            sgl.append(S.add("gpsimd", lambda e, k=k, a=src_slice(k), b=shape[1]: e.dma_start(
                out=img_slabs[k].rearrange("p (a b) -> p a b", b=b), in_=a),
                inc=SG, dma=True))
        return sgl

    p_f1 = []
    p_f2 = []
    for p, (w1s, w2s) in enumerate([(sw1, sw2), (w1[0], w2[0]), (w1[1], w2[1])]):
        p_f1.append(img_plain(
            [t_f1[p][hs] for hs in range(8)], 8, (8, 512),
            lambda hs, s=w1s: s[:, hs * 512:(hs + 1) * 512]
            .rearrange("(dc q) h -> q dc h", q=128)))
        p_f2.append(img_plain(
            [t_f2[p][i] for i in range(8)], 8, (8, 512),
            lambda i, s=w2s: s[(i % 4) * 1024:(i % 4 + 1) * 1024,
                               (i // 4) * 512:(i // 4 + 1) * 512]
            .rearrange("(hc q) d -> q hc d", q=128)))

    def ln_first(s, tile, waits):
        S.add("vector", lambda e, s=s, t=tile: e.reduce_sum(
            out=red[:, s:s + 1], in_=t, axis=AX),
            waits=list(waits), drain=True)
        return S.add("scalar", lambda e, s=s, t=tile: e.activation(
            out=sq_scr[:], in_=t, func=AF.Square, accum_out=sqs[:, s:s + 1]),
            waits=list(waits), inc=AC)

    def ln_rest(n, rms, a_sq):
        d_var, a_sd1, d_t3, a_sd2 = [0] * n, [0] * n, [0] * n, [0] * n
        done = [0] * n
        for s in range(n):
            S.add("vector", lambda e, s=s: e.tensor_scalar_mul(
                out=mu[:, s:s + 1], in0=red[:, s:s + 1], scalar1=1.0 / D))
        for s in range(n):
            S.add("vector", lambda e, s=s: e.tensor_mul(
                out=mu2[:, s:s + 1], in0=mu[:, s:s + 1], in1=mu[:, s:s + 1]),
                waits=[(AC, a_sq[s])])
        for s in range(n):
            d_var[s] = S.add("vector", lambda e, s=s: e.scalar_tensor_tensor(
                out=var[:, s:s + 1], in0=sqs[:, s:s + 1], scalar=1.0 / D,
                in1=mu2[:, s:s + 1], op0=OP.mult, op1=OP.subtract),
                inc=DV, drain=(s == n - 1))
        for s in range(n):
            a_sd1[s] = S.add("scalar", lambda e, s=s: e.activation(
                out=sd1[:, s:s + 1], in_=var[:, s:s + 1], func=AF.Sqrt,
                bias=eps_t[:, 0:1]), waits=[(DV, d_var[s]), (GP, g_id)], inc=AC)
        d_rc = [0] * n
        for s in range(n):
            d_rc[s] = S.add("vector", lambda e, s=s: e.reciprocal(
                out=r1t[:, s:s + 1], in_=sd1[:, s:s + 1]),
                waits=[(AC, a_sd1[s])], inc=DV, drain=(s == n - 1))
        if not rms:
            return d_rc
        for s in range(n):
            S.add("vector", lambda e, s=s: e.tensor_mul(
                out=t2t[:, s:s + 1], in0=var[:, s:s + 1], in1=r1t[:, s:s + 1]))
        for s in range(n):
            d_t3[s] = S.add("vector", lambda e, s=s: e.tensor_mul(
                out=t3t[:, s:s + 1], in0=t2t[:, s:s + 1], in1=r1t[:, s:s + 1]),
                inc=DV, drain=(s == n - 1))
        for s in range(n):
            a_sd2[s] = S.add("scalar", lambda e, s=s: e.activation(
                out=sd2[:, s:s + 1], in_=t3t[:, s:s + 1], func=AF.Sqrt,
                bias=eps_t[:, 0:1]), waits=[(DV, d_t3[s])], inc=AC)
        for s in range(n):
            S.add("vector", lambda e, s=s: e.reciprocal(
                out=r2t[:, s:s + 1], in_=sd2[:, s:s + 1]),
                waits=[(AC, a_sd2[s])], drain=(s == n - 1))
        for s in range(n):
            S.add("vector", lambda e, s=s: e.tensor_mul(
                out=sig[:, s:s + 1], in0=r1t[:, s:s + 1], in1=r2t[:, s:s + 1]))
        for s in range(n):
            S.add("vector", lambda e, s=s: e.tensor_scalar_add(
                out=s1t[:, s:s + 1], in0=r1t[:, s:s + 1], scalar1=1.0))
        for s in range(n):
            done[s] = S.add("vector", lambda e, s=s: e.tensor_mul(
                out=s2t[:, s:s + 1], in0=mu[:, s:s + 1], in1=r1t[:, s:s + 1]),
                inc=DV, drain=(s == n - 1))
        return done

    def ln2_sub_chain(s, a_sq_s):
        S.add("vector", lambda e, s=s: e.tensor_scalar_mul(
            out=mu[:, s:s + 1], in0=red[:, s:s + 1], scalar1=1.0 / D),
            drain=True)
        S.add("vector", lambda e, s=s: e.tensor_mul(
            out=mu2[:, s:s + 1], in0=mu[:, s:s + 1], in1=mu[:, s:s + 1]),
            waits=[(AC, a_sq_s)], drain=True)
        d_var = S.add("vector", lambda e, s=s: e.scalar_tensor_tensor(
            out=var[:, s:s + 1], in0=sqs[:, s:s + 1], scalar=1.0 / D,
            in1=mu2[:, s:s + 1], op0=OP.mult, op1=OP.subtract),
            inc=DV, drain=True)
        a_sd1 = S.add("scalar", lambda e, s=s: e.activation(
            out=sd1[:, s:s + 1], in_=var[:, s:s + 1], func=AF.Sqrt,
            bias=eps_t[:, 0:1]), waits=[(DV, d_var), (GP, g_id)], inc=AC)
        return S.add("vector", lambda e, s=s: e.reciprocal(
            out=r1t[:, s:s + 1], in_=sd1[:, s:s + 1]),
            waits=[(AC, a_sd1)], inc=DV, drain=True)

    def transposes_to_tT(src, sub, wait_dve, scale_cols, wait_ac=None):
        par = sub % 2
        w = [(GP, g_id), (PE, state["h_pe"])]
        if wait_dve is not None:
            w.append((DV, wait_dve))
        if wait_ac is not None:
            w.append((AC, wait_ac))
        if state["pT_free"][par] is not None:
            w += state["pT_free"][par]
        t = None
        for dc in range(8):
            t = S.add("tensor", lambda e, d=dc, p=par, s=src: e.transpose(
                pT[p][:, d * 128:(d + 1) * 128], s[:, d * 128:(d + 1) * 128], idb[:]),
                waits=w if dc == 0 else (), inc=PE if dc == 7 else None)
        state["h_pe"] = t
        eng, SEM = ("vector", DV) if sub % 2 == 0 else ("scalar", AC)
        if scale_cols is None:
            if eng == "vector":
                d_ev = S.add(eng, lambda e, p=par: e.tensor_copy(
                    out=tT_sb[:, :, sub * 128:(sub + 1) * 128],
                    in_=pT[p].rearrange("p (a b) -> p a b", b=128)),
                    waits=[(PE, t)], inc=SEM)
            else:
                d_ev = S.add(eng, lambda e, p=par: e.copy(
                    out=tT_sb[:, :, sub * 128:(sub + 1) * 128],
                    in_=pT[p].rearrange("p (a b) -> p a b", b=128)),
                    waits=[(PE, t)], inc=SEM)
        else:
            for dc in range(8):
                if eng == "vector":
                    d_ev = S.add(eng, lambda e, d=dc, p=par: e.tensor_scalar_mul(
                        out=tT_sb[:, d, sub * 128:(sub + 1) * 128],
                        in0=pT[p][:, d * 128:(d + 1) * 128],
                        scalar1=scale_cols[:, d:d + 1]),
                        waits=[(PE, t)] if dc == 0 else (), inc=SEM if dc == 7 else None)
                else:
                    d_ev = S.add(eng, lambda e, d=dc, p=par: e.activation(
                        out=tT_sb[:, d, sub * 128:(sub + 1) * 128],
                        in_=pT[p][:, d * 128:(d + 1) * 128],
                        func=AF.Copy, scale=scale_cols[:, d:d + 1]),
                        waits=[(PE, t)] if dc == 0 else (), inc=SEM if dc == 7 else None)
        evs = [(SEM, d_ev)]
        state["pT_free"][par] = evs
        return evs

    def emit_ln1(c):
        fw = [(SW, state["g_x"][c])]
        a_sq = [ln_first(s, x_sb[:, s, :], fw) for s in range(NSUB)]
        done = ln_rest(NSUB, True, a_sq)
        d_hs = []
        for s in range(NSUB):
            wh = [(DV, done[s])]
            d_h = S.add("vector", lambda e, s=s: e.tensor_scalar(
                out=hslot(s)[:], in0=x_sb[:, s, :], scalar1=mu[:, s:s + 1],
                scalar2=sig[:, s:s + 1], op0=OP.subtract, op1=OP.mult),
                waits=wh, inc=DV)
            S.add("vector", lambda e, s=s: e.tensor_scalar(
                out=x_sb[:, s, :], in0=x_sb[:, s, :], scalar1=s1t[:, s:s + 1],
                scalar2=s2t[:, s:s + 1], op0=OP.mult, op1=OP.subtract))
            d_hs.append(d_h)
        return d_hs

    def emit_ln1_transposes(d_hs):
        dvm, acm = 0, 0
        for sub in range(NSUB):
            for sem, val in transposes_to_tT(hslot(sub), sub, d_hs[sub],
                                             scale_cols=wn_col):
                if sem is DV:
                    dvm = max(dvm, val)
                else:
                    acm = max(acm, val)
        return [(DV, dvm), (AC, acm)]

    d_hs0 = emit_ln1(0)
    state["d_hT"] = emit_ln1_transposes(d_hs0)

    d_bdt = vec_to_cols(b_dt[None, :], INNER, bdt_col[:], 16, waits=[(DV, d_vrow)])
    d_b1 = 0
    for p, src in enumerate([sb1, b1[0], b1[1]]):
        d_b1 = vec_to_cols(src[None, :], HID, b1c[:, :, p], 48 + 32 * p,
                           waits=[(DV, d_bdt if p == 0 else d_b1)])
    d_g0 = vec_to_cols(gate_w[0][None, :], D, gd0[:], 144, waits=[(DV, d_b1)])
    d_g1 = vec_to_cols(gate_w[1][None, :], D, gd0[:], 152, waits=[(DV, d_g0)])
    d_gd = S.add("vector", lambda e: e.tensor_tensor(
        out=gdiff[:], in0=gd0[:], in1=pM[:, 144:152], op=OP.subtract),
        waits=[(DV, d_g1)], inc=DV)
    state["u_war_pe"] = PE.n

    for c in range(NCHUNK):
        b = c // KCH
        kc = c % KCH
        k0 = kc * TC

        d_hT = state["d_hT"]

        act_uz_wait = [(PE, max(state["u_war_pe"], state["prev_zs_reader"])),
                       (DV, d_gd)]
        first_uz = True
        for es in range(8):
            if c == 0:
                srcs = [W_in[es * 512 + ui * 256:es * 512 + (ui + 1) * 256, :]
                        .rearrange("(a q) d -> q a d", q=128) for ui in range(2)]
                slot_c, wview, mw, st = build_slab_c0(
                    "a1", pool_a1, srcs, 512, g_src_q, g_out_q, t_a1[es])
                state["st_a1"] = st
                wbase = mw
            else:
                wld = [(SS, state["st_a1"])] if c == 1 else []
                slot_c, wview, t_w = ring_load(t_a1[es], (8, 512), waits=wld)
                wbase = [(SW, t_w)]
            last_pe = None
            for el in range(4):
                ec = es * 4 + el
                par = ec % 3
                w = wbase + d_hT
                if state["pA_free"][par] is not None:
                    w.append(state["pA_free"][par])
                for dc in range(8):
                    t_mm = S.add("tensor", lambda e, d=dc, p=par, v=wview, el=el: e.matmul(
                        pA[p][:], v[:, d, el * 128:(el + 1) * 128],
                        tT_sb[:, d, :], start=(d == 0), stop=(d == 7)),
                        waits=w if dc == 0 else (), inc=PE if dc == 7 else None)
                last_pe = t_mm
                tgt = (u_sb, ec) if ec < 16 else (zs_sb, ec - 16)
                d_ev = S.add("scalar", lambda e, p=par, t=tgt: e.activation(
                    out=t[0][:, t[1], :], in_=pA[p][:], func=AF.Silu),
                    waits=[(PE, t_mm)] + (act_uz_wait if first_uz else []), inc=AC)
                first_uz = False
                state["pA_free"][par] = (AC, d_ev)
            ring_set_free(slot_c, PE, last_pe)
        a_uz_all = AC.n

        if c == 0:
            pool_a23[2]["free"] = (PE, PE.n)

        a_lam = []
        bp_emitted = [0]
        scan_emitted = [0]
        d_bp = [0] * 16
        d_m_l = [0] * 16

        def emit_bp(upto, a_lam=a_lam, bp_emitted=bp_emitted, d_bp=d_bp):
            while bp_emitted[0] < min(upto, 16):
                ic = bp_emitted[0]
                d_bp[ic] = S.add("vector", lambda e, i=ic: e.scalar_tensor_tensor(
                    out=b_scr2[i % 2][:],
                    in0=lam_v[:, i, :], scalar=1.0, in1=u_sb[:, i, :],
                    op0=OP.subtract, op1=OP.mult),
                    waits=[(AC, a_lam[ic])], inc=DV)
                bp_emitted[0] += 1

        def emit_scan_ops(upto, kc=kc, scan_emitted=scan_emitted, d_bp=d_bp,
                          d_m_l=d_m_l):
            emit_bp(upto + 1)
            while scan_emitted[0] < upto:
                ic = scan_emitted[0]
                init = 0.0 if kc == 0 else carry[:, ic:ic + 1]
                S.add("vector", lambda e, i=ic, init=init: e.tensor_tensor_scan(
                    out=s_v[:, i, :], data0=lam_v[:, i, :],
                    data1=b_scr2[i % 2][:],
                    initial=init, op0=OP.mult, op1=OP.add),
                    waits=[(DV, d_bp[ic])], drain=True)
                d_m_l[ic] = S.add("vector", lambda e, i=ic: e.tensor_mul(
                    out=zs_sb[:, i, :], in0=s_v[:, i, :], in1=zs_sb[:, i, :]),
                    inc=DV)
                scan_emitted[0] += 1

        for js in range(8):
            if c == 0:
                srcs = [W_dt[js * 256 + jl * 128:js * 256 + (jl + 1) * 128, :]
                        .rearrange("q (a d) -> q a d", d=1024) for jl in range(2)]
                slot_c, wview, mw, st = build_slab_c0(
                    "a23", pool_a23, srcs, 256, g_src_h, g_out_h, t_a2[js])
                state["st_a2"] = st
                wbase = mw
            else:
                wld = [(SS, state["st_a2"])] if c == 1 else []
                slot_c, wview, t_w = ring_load(t_a2[js], (16, 256), waits=wld)
                wbase = [(SW, t_w)]
            last_pe = None
            for jl in range(2):
                j = js * 2 + jl
                par = j % 3
                w = wbase + [(AC, a_uz_all)]
                if state["pA_free"][par] is not None:
                    w.append(state["pA_free"][par])
                for ic in range(16):
                    t_mm = S.add("tensor", lambda e, i=ic, p=par, v=wview, jl=jl: e.matmul(
                        pA[p][:], v[:, i, jl * 128:(jl + 1) * 128],
                        u_sb[:, i, :], start=(i == 0), stop=(i == 15)),
                        waits=w if ic == 0 else (), inc=PE if ic == 15 else None)
                last_pe = t_mm
                wlam = [(PE, t_mm)]
                if c > 0:
                    wlam.append((SO, state["out_t"]))
                a_ev = S.add("scalar", lambda e, p=par, j=j: e.activation(
                    out=lam_v[:, j, :], in_=pA[p][:], func=AF.Sigmoid,
                    bias=bdt_col[:, j:j + 1]), waits=wlam, inc=AC)
                a_lam.append(a_ev)
                state["pA_free"][par] = (AC, a_ev)
                emit_scan_ops(len(a_lam) - 1)
            ring_set_free(slot_c, PE, last_pe)
        pe_a2_all = PE.n
        emit_scan_ops(16)
        d_m_all = DV.n
        for ic in range(16):
            S.add("vector", lambda e, i=ic: e.tensor_copy(
                out=carry[:, i:i + 1], in_=s_v[:, i, TC - 1:TC]))

        S.add("scalar", lambda e: e.activation(
            out=sd1[:, 0:1], in_=eps_t[:, 0:1], func=AF.Sqrt),
            waits=[(GP, g_id)])
        if c == 0:
            slabs = []
            dvm, acm = 0, 0
            for ks in range(4):
                dh, ih = ks // 2, ks % 2
                srcs = [W_out[dh * 512 + up * 256:dh * 512 + (up + 1) * 256,
                              ih * 1024:(ih + 1) * 1024]
                        .rearrange("(a q) i -> q a i", q=128) for up in range(2)]
                cell, iv, mw, st = build_slab_c0(
                    "a23", pool_a23, srcs, 512, g_src_q, g_out_q, t_c[ks])
                state["st_c"] = st
                dvm, acm = max(dvm, mw[0][1]), max(acm, mw[1][1])
                slabs.append((cell, iv, None))
            wC = [(DV, dvm), (AC, acm)]
        else:
            wld = [(SS, state["st_c"])] if c == 1 else []
            slabs = [ring_load(t_c[k], (8, 512), waits=wld) for k in range(4)]
            wC = [(SW, slabs[3][2])]
        first_cg = True
        d_g2 = [0] * NSUB
        for sub in range(NSUB):
            for dh in range(2):
                par = (2 * sub + dh) % 2
                w = wC + ([] if first_cg else [(DV, d_m_all)])
                if state["pB_free"][par] is not None:
                    w.append(state["pB_free"][par])
                t_mm = None
                for ih in range(2):
                    v = slabs[2 * dh + ih][1]
                    for il in range(8):
                        ic = ih * 8 + il
                        mmw = list(w) if ic == 0 else []
                        if first_cg:
                            mmw.append((DV, d_m_l[ic]))
                        t_mm = S.add("tensor",
                                     lambda e, i=ic, il=il, v=v, p=par, sub=sub,
                                     first=(ic == 0), last=(ic == 15):
                                     e.matmul(pB[p][:],
                                              zs_sb[:, i, sub * 128:(sub + 1) * 128],
                                              v[:, il, :], start=first, stop=last),
                                     waits=mmw, inc=PE if ic == 15 else None)
                first_cg = False
                state["pe_c_mms"] = t_mm
                d_x1 = S.add("vector", lambda e, p=par, sub=sub, dh=dh: e.tensor_tensor(
                    out=x1_sb[:, sub, dh * 512:(dh + 1) * 512],
                    in0=x_sb[:, sub, dh * 512:(dh + 1) * 512],
                    in1=pB[p][:], op=OP.subtract),
                    waits=[(PE, t_mm)], inc=DV)
                state["pB_free"][par] = (DV, d_x1)
                if dh == 1:
                    state[f"d_x1_{sub}"] = d_x1
                    state["d_x1_last"] = d_x1
                    a_sq_s = ln_first(sub, x1_sb[:, sub, :], [(DV, d_x1)])
                    if sub < NSUB - 1:
                        d_g2[sub] = ln2_sub_chain(sub, a_sq_s)
                    else:
                        sq_last = a_sq_s
        for cell, _, _ in slabs:
            ring_set_free(cell, PE, state["pe_c_mms"])
        state["prev_zs_reader"] = state["pe_c_mms"]

        d_gv = [0] * NSUB
        for sub in range(NSUB - 1):
            d_gv[sub] = S.add("vector", lambda e, s=sub: e.tensor_scalar(
                out=h4[:, s, :], in0=x1_sb[:, s, :], scalar1=mu[:, s:s + 1],
                scalar2=r1t[:, s:s + 1], op0=OP.subtract, op1=OP.mult),
                waits=[(DV, d_g2[sub]), (PE, state["h_pe"])], inc=DV)
        ls = NSUB - 1
        d_g2[ls] = ln2_sub_chain(ls, sq_last)
        d_gv[ls] = S.add("vector", lambda e, s=ls: e.tensor_scalar(
            out=h4[:, s, :], in0=x1_sb[:, s, :], scalar1=mu[:, s:s + 1],
            scalar2=r1t[:, s:s + 1], op0=OP.subtract, op1=OP.mult),
            waits=[(DV, d_g2[ls]), (PE, state["h_pe"])], inc=DV)
        dvm, acm = 0, 0
        for sub in range(NSUB):
            for sem, val in transposes_to_tT(h4[:, sub, :], sub, d_gv[sub],
                                             scale_cols=None):
                if sem is DV:
                    dvm = max(dvm, val)
                else:
                    acm = max(acm, val)
        d_gT = [(DV, dvm), (AC, acm)]

        d_hs_next = None
        if c + 1 < NCHUNK:
            b2_, kc2 = (c + 1) // KCH, (c + 1) % KCH
            state["g_x"][c + 1] = S.add("sync", lambda e, b=b2_, k0=kc2 * TC: e.dma_start(
                out=x_sb[:], in_=x_in[b, k0:k0 + TC, :].rearrange("(s p) d -> p s d", p=128)),
                waits=[(DV, state["d_x1_last"])], inc=SW, dma=True)
            d_hs_next = emit_ln1(c + 1)

        t_sd = None
        for dc in range(8):
            t_sd = S.add("tensor", lambda e, d=dc: e.matmul(
                pM[0:1, :], gdiff[:, d:d + 1], tT_sb[:, d, :],
                start=(d == 0), stop=(d == 7)),
                waits=d_gT + [(DV, d_gd)] if dc == 0 else (),
                inc=PE if dc == 7 else None)

        def emit_mask_section():
            S.add("vector", lambda e: e.tensor_single_scalar(
                out=se0[:], in_=pM[0:1, :], scalar=0.0, op=OP.is_gt),
                waits=[(PE, t_sd)])
            d_mrow = S.add("vector", lambda e: e.tensor_copy(out=mrow[:], in_=se0[:]), inc=DV)
            t_mrep = S.add("tensor", lambda e: e.matmul(
                pM[:, :], ones_row[:], mrow[:], start=True, stop=True),
                waits=[(DV, d_mrow)], inc=PE)
            d_mrep = S.add("vector", lambda e: e.tensor_copy(out=mrep_sb[:], in_=pM[:]),
                           waits=[(PE, t_mrep)], inc=DV)
            d_mrep0 = S.add("vector", lambda e: e.tensor_scalar(
                out=mrep0_sb[:], in0=mrep_sb[:], scalar1=-1.0, scalar2=1.0,
                op0=OP.mult, op1=OP.add), inc=DV)
            return d_mrep, d_mrep0

        pe_f2_prev = 0
        for p in range(3):
            if p == 0:
                rhs_view, d_rhs = tT_sb, list(d_gT)
                d_rhs_dc = None
            else:
                msk = mrep0_sb if p == 1 else mrep_sb
                d_msk = d_mrep0 if p == 1 else d_mrep
                d_rhs_dc = []
                for dc in range(8):
                    wm = [(DV, d_msk)]
                    if p == 2:
                        wm.append((PE, state["pe_f1_done"]))
                    d_b = S.add("vector", lambda e, d=dc, m=msk: e.tensor_mul(
                        out=gTe_v[:, d, :], in0=tT_sb[:, d, :], in1=m[:]),
                        waits=wm if dc == 0 else (), inc=DV)
                    d_rhs_dc.append(d_b)
                d_rhs = [(DV, d_rhs_dc[7])]
                rhs_view = gTe_v
            sl2_next = []
            first_f1 = True
            for hs in range(8):
                wslab = [(SG, p_f1[p][hs])] if c == 0 else []
                slot_c, wview, t_w = ring_load(t_f1[p][hs], (8, 512), waits=wslab)
                if hs >= 4:
                    sl2_next.append(ring_load(
                        t_f2[p][hs - 4], (8, 512),
                        waits=[(SG, p_f2[p][hs - 4])] if c == 0 else ()))
                last_pe = None
                for hl in range(4):
                    hc = hs * 4 + hl
                    par = hc % 3
                    per_dc = d_rhs_dc is not None and hs == 0 and hl == 0
                    w = [(SW, t_w)] + ([] if per_dc else d_rhs)
                    if first_f1:
                        w.append((PE, pe_f2_prev))
                        first_f1 = False
                    if state["pA_free"][par] is not None:
                        w.append(state["pA_free"][par])
                    for dc in range(8):
                        mmw = list(w) if dc == 0 else []
                        if per_dc:
                            mmw.append((DV, d_rhs_dc[dc]))
                        t_mm = S.add("tensor", lambda e, d=dc, p2=par, v=wview, hl=hl, rv=rhs_view:
                                     e.matmul(pA[p2][:], v[:, d, hl * 128:(hl + 1) * 128],
                                              rv[:, d, :], start=(d == 0), stop=(d == 7)),
                                     waits=mmw, inc=PE if dc == 7 else None)
                    last_pe = t_mm
                    d_ev = S.add("scalar", lambda e, p2=par, hc=hc, p=p: e.activation(
                        out=hrelu_v[:, hc, :], in_=pA[p2][:], func=AF.Relu,
                        bias=b1c[:, hc:hc + 1, p]), waits=[(PE, t_mm)], inc=AC)
                    state["pA_free"][par] = (AC, d_ev)
                ring_set_free(slot_c, PE, last_pe)
            a_f1_all = AC.n
            state["pe_f1_done"] = PE.n
            if p == 0:
                d_mrep, d_mrep0 = emit_mask_section()

            if p == 2 and d_hs_next is not None:
                state["d_hT"] = emit_ln1_transposes(d_hs_next)
                state["u_war_pe"] = PE.n

            banks = [pA[0], pA[1], pA[2], pB[0]]
            for dq in range(2):
                sl2 = sl2_next
                sl2_next = []
                first_pending = [None] * 4
                for sub in range(4):
                    w = [(AC, a_f1_all)]
                    fr = state["pA_free"][sub] if sub < 3 else state["pB_free"][0]
                    if fr is not None:
                        w.append(fr)
                    if p == 0:
                        w.append((DV, d_vrow))
                        S.add("tensor", lambda e, b=banks[sub], dq=dq: e.matmul(
                            b[:], ones_row[:], vrow[0:1, dq * 512:(dq + 1) * 512],
                            start=True, stop=False), waits=w)
                    else:
                        first_pending[sub] = w
                t_stop = [0] * 4
                for ks in range(4):
                    cell, v, t_w = sl2[ks]
                    if dq == 0:
                        sl2_next.append(ring_load(
                            t_f2[p][4 + ks], (8, 512),
                            waits=[(SG, p_f2[p][4 + ks])] if c == 0 else ()))
                    for hl in range(8):
                        hc = ks * 8 + hl
                        for sub in range(4):
                            mmw = [(SW, t_w)] if (hl == 0 and sub == 0) else []
                            st = False
                            if first_pending[sub] is not None:
                                mmw = mmw + first_pending[sub]
                                first_pending[sub] = None
                                st = True
                            last = hc == 31
                            t_mm = S.add("tensor",
                                         lambda e, b=banks[sub], hc=hc, hl=hl,
                                         v=v, sub=sub, st=st, last=last:
                                         e.matmul(b[:],
                                                  hrelu_v[:, hc, sub * 128:(sub + 1) * 128],
                                                  v[:, hl, :], start=st, stop=last),
                                         waits=mmw,
                                         inc=PE if (last or (hl == 7 and sub == 3)) else None)
                            if last:
                                t_stop[sub] = t_mm
                    ring_set_free(cell, PE, PE.n)
                for sub in range(4):
                    if p == 0:
                        d_ev = S.add("vector", lambda e, b=banks[sub], sub=sub, dq=dq:
                                     e.tensor_tensor(
                                         out=facc_v[:, sub, dq * 512:(dq + 1) * 512],
                                         in0=x1_sb[:, sub, dq * 512:(dq + 1) * 512],
                                         in1=b[:], op=OP.add),
                                     waits=[(PE, t_stop[sub])], inc=DV)
                    else:
                        d_ev = S.add("vector", lambda e, b=banks[sub], sub=sub, dq=dq:
                                     e.tensor_tensor(
                                         out=facc_v[:, sub, dq * 512:(dq + 1) * 512],
                                         in0=facc_v[:, sub, dq * 512:(dq + 1) * 512],
                                         in1=b[:], op=OP.add),
                                     waits=[(PE, t_stop[sub])], inc=DV)
                    if sub < 3:
                        state["pA_free"][sub] = (DV, d_ev)
                    else:
                        state["pB_free"][0] = (DV, d_ev)
            pe_f2_prev = PE.n
        d_facc = DV.n

        state["out_t"] = S.add("gpsimd", lambda e, b=b, k0=k0: e.dma_start(
            out=y_out[b, k0:k0 + TC, :].rearrange("(s p) d -> p s d", p=128),
            in_=facc_v[:]),
            waits=[(DV, d_facc)], inc=SO, dma=True)

    S.add("gpsimd", lambda e: None, waits=[(SO, SO.n)])

    @block.sync
    def _(eng):
        S.replay("sync", eng)

    @block.tensor
    def _(eng):
        S.replay("tensor", eng)

    @block.vector
    def _(eng):
        S.replay("vector", eng)

    @block.scalar
    def _(eng):
        S.replay("scalar", eng)

    @block.gpsimd
    def _(eng):
        S.replay("gpsimd", eng)

    stack.close()
    return nc


_NC_CACHE = None


def kernel(**inputs):
    global _NC_CACHE
    if _NC_CACHE is None:
        _NC_CACHE = build_program()
    nc = _NC_CACHE

    x = np.ascontiguousarray(inputs["L0_attn_x"])
    shards = np.split(x, NCORES, axis=0)
    base = {k: np.ascontiguousarray(v) for k, v in inputs.items() if k != "L0_attn_x"}
    in_maps = []
    for c in range(NCORES):
        m = dict(base)
        m["L0_attn_x"] = shards[c]
        in_maps.append(m)
    res = run_bass_kernel_spmd(nc, in_maps, core_ids=list(range(NCORES)))
    out = np.concatenate([np.asarray(r["y"]) for r in res.results], axis=0)
    return out.astype(np.float32)
